# version 06
# baseline (speedup 1.0000x reference)
import numpy as np
import ml_dtypes
from contextlib import ExitStack
import concourse.bass as bass
import concourse.mybir as mybir
from concourse.bass_utils import run_bass_kernel_spmd

F32 = mybir.dt.float32
BF16 = mybir.dt.bfloat16
AF = mybir.ActivationFunctionType
ALU = mybir.AluOpType
AX = mybir.AxisListType

D = 4096
NT = 1024
NTH = 1026
KC_D = 32
W1 = 1024
NHD = 16
DH = 64
INT = 8192
LAT = 2048
NEG = -1.0e9
EPS = 1e-5
N_CORES = 8


def build_program(taps=()):
    nc = bass.Bass("TRN2", target_bir_lowering=False, detect_race_conditions=False)

    x_h = nc.dram_tensor("x_fm", [D, NTH], F32, kind="ExternalInput").ap()
    convw_h = nc.dram_tensor("conv_w", [3, D, 3 * W1], F32, kind="ExternalInput").ap()
    convb_h = nc.dram_tensor("conv_b_pm", [128, 24], F32, kind="ExternalInput").ap()
    wq_h = nc.dram_tensor("wq", [W1, NHD, DH], F32, kind="ExternalInput").ap()
    bq_h = nc.dram_tensor("bq_pm", [128, 8], F32, kind="ExternalInput").ap()
    wk_h = nc.dram_tensor("wk", [W1, NHD, DH], F32, kind="ExternalInput").ap()
    bk_h = nc.dram_tensor("bk_pm", [128, 8], F32, kind="ExternalInput").ap()
    wv_h = nc.dram_tensor("wv", [W1, NHD, DH], F32, kind="ExternalInput").ap()
    bv_h = nc.dram_tensor("bv", [NHD, DH], F32, kind="ExternalInput").ap()
    wo_h = nc.dram_tensor("wo", [NHD, DH, W1], F32, kind="ExternalInput").ap()
    bo_h = nc.dram_tensor("bo_pm", [128, 8], F32, kind="ExternalInput").ap()
    projw_h = nc.dram_tensor("proj_w", [W1, D], F32, kind="ExternalInput").ap()
    projb_h = nc.dram_tensor("proj_b_pm", [128, 32], F32, kind="ExternalInput").ap()
    gatew_h = nc.dram_tensor("gate_w_pm", [128, 256], F32, kind="ExternalInput").ap()
    upw_h = nc.dram_tensor("up_w", [D, INT], F32, kind="ExternalInput").ap()
    downw_h = nc.dram_tensor("down_w", [INT, D], F32, kind="ExternalInput").ap()
    fc1w_h = nc.dram_tensor("fc1_w", [D, LAT], F32, kind="ExternalInput").ap()
    fc2w_h = nc.dram_tensor("fc2_w", [LAT, D], F32, kind="ExternalInput").ap()
    expw_h = nc.dram_tensor("expert_w", [LAT, LAT], F32, kind="ExternalInput").ap()
    lng_h = nc.dram_tensor("ln_gamma_pm", [128, 32], F32, kind="ExternalInput").ap()
    lnb_h = nc.dram_tensor("ln_beta_pm", [128, 32], F32, kind="ExternalInput").ap()

    out_h = nc.dram_tensor("out_fm", [D, NT], F32, kind="ExternalOutput").ap()

    sa_h = nc.dram_tensor("scr_sa", [D, NT], F32, kind="Internal").ap()
    sf_h = nc.dram_tensor("scr_sf", [D, NT], F32, kind="Internal").ap()
    esc_h = nc.dram_tensor("scr_esc", [LAT, NT], BF16, kind="Internal").ap()
    wosp_h = nc.dram_tensor("scr_wo", [W1, NT], BF16, kind="Internal").ap()

    taps = set(taps)
    tap_h = {}

    def tap_decl(name, shape, dt=F32):
        if name in taps:
            tap_h[name] = nc.dram_tensor("tap_" + name, list(shape), dt,
                                         kind="ExternalOutput").ap()

    tap_decl("xn", [D, NTH], BF16)
    tap_decl("qkv", [3 * W1, NT], BF16)
    tap_decl("qh", [W1, NT], BF16)
    tap_decl("kh", [W1, NT], BF16)
    tap_decl("vh", [8 * 128, W1], BF16)
    tap_decl("o", [W1, NT], BF16)
    tap_decl("wo_out", [W1, NT], BF16)
    tap_decl("route", [128, NT])
    tap_decl("xlat", [LAT, NT], BF16)
    tap_decl("esc", [LAT, NT], BF16)
    tap_decl("h", [INT // 2, NT], BF16)
    tap_decl("sa", [D, NT])
    tap_decl("sf", [D, NT])

    mtri = np.zeros((128, 128), np.float32)
    mtri[np.triu_indices(128, 1)] = NEG
    mask4_h = nc.inline_tensor(np.tile(mtri, (1, 4)).astype(np.float32), name="mask4_c").ap()
    idb_h = nc.inline_tensor(np.eye(128, dtype=ml_dtypes.bfloat16), name="ident_bf").ap()
    idf_h = nc.inline_tensor(np.eye(128, dtype=np.float32), name="ident_f32").ap()
    onesb_h = nc.inline_tensor(np.ones((128, 1), ml_dtypes.bfloat16), name="ones_col").ap()
    _ev = np.zeros((128, 1), np.float32); _ev[0:64] = 1.0
    evmask_h = nc.inline_tensor(_ev, name="evmask_c").ap()
    odmask_h = nc.inline_tensor(1.0 - _ev, name="odmask_c").ap()
    evsc_h = nc.inline_tensor((_ev * 0.125).astype(np.float32), name="evsc_c").ap()
    odsc_h = nc.inline_tensor(((1.0 - _ev) * 0.125).astype(np.float32), name="odsc_c").ap()
    onesf_h = nc.inline_tensor(np.ones((1, 128), np.float32), name="ones_row").ap()

    convw_v = [convw_h[w].rearrange("(kc p) m -> p kc m", p=128) for w in range(3)]
    wq_v = wq_h.rearrange("(kc p) h e -> p kc (h e)", p=128)
    wk_v = wk_h.rearrange("(kc p) h e -> p kc (h e)", p=128)
    wv_v = wv_h.rearrange("(kc p) h e -> p kc (h e)", p=128)
    wo_v = wo_h.rearrange("(kc h2) e w -> (h2 e) kc w", h2=2)
    projw_v = projw_h.rearrange("(kc p) m -> p kc m", p=128)
    upw_v = upw_h.rearrange("(kc p) m -> p kc m", p=128)
    downw_v = downw_h.rearrange("(kc p) m -> p kc m", p=128)
    fc1w_v = fc1w_h.rearrange("(kc p) m -> p kc m", p=128)
    fc2w_v = fc2w_h.rearrange("(kc p) m -> p kc m", p=128)
    expw_v = expw_h.rearrange("(kc p) m -> p kc m", p=128)

    x_v = x_h.rearrange("(mc p) t -> mc p t", p=128)
    sa_v = sa_h.rearrange("(mc p) t -> mc p t", p=128)
    sf_v = sf_h.rearrange("(mc p) t -> mc p t", p=128)
    esc_v = esc_h.rearrange("(mc p) t -> p mc t", p=128)
    wosp_v = wosp_h.rearrange("(mc p) t -> mc p t", p=128)
    wosp_r = wosp_h.rearrange("(mc p) t -> p mc t", p=128)
    out_v = out_h.rearrange("(mc p) t -> mc p t", p=128)

    root = ExitStack()
    sbuf = lambda name, shape, dt: root.enter_context(nc.sbuf_tensor(name, shape, dt))
    mksem = lambda name: root.enter_context(nc.semaphore(name))

    arA = sbuf("arA", [128, KC_D * NTH], BF16)
    arB = sbuf("arB", [128, 43008], BF16)
    slot = [sbuf(f"wslot{i}", [128, 8 * 512], BF16) for i in range(2)]
    gate_sb = sbuf("gate_sb", [128, 256], BF16)
    idb = sbuf("idb", [128, 128], BF16)
    onesb = sbuf("onesb", [128, 1], BF16)
    xf_sl = [sbuf(f"xf{i}", [128, NTH], F32) for i in range(2)]
    spw = [sbuf(f"spw{i}", [128, NT], F32) for i in range(2)]
    tsc = [sbuf(f"tsc{i}", [128, NTH], F32) for i in range(2)]
    mneg_bc = sbuf("mneg_bc", [128, NTH], F32)
    rstd_bc = sbuf("rstd_bc", [128, NTH], F32)
    mask1 = sbuf("mask1", [128, 128], F32)
    idf = sbuf("idf", [128, 128], F32)
    onesf = sbuf("onesf", [1, 128], F32)
    evsc = sbuf("evsc", [128, 1], F32)
    odsc = sbuf("odsc", [128, 1], F32)
    evm = sbuf("evm", [128, 1], F32)
    odm = sbuf("odm", [128, 1], F32)
    bq_e = sbuf("bq_e", [128, 8], F32)
    bq_o = sbuf("bq_o", [128, 8], F32)
    route_cols = sbuf("route_cols", [128, 8], F32)
    gsb = sbuf("gsb", [128, 8], F32)
    tg = sbuf("tg", [128, 8], F32)
    tg2 = sbuf("tg2", [128, 1], F32)
    tg3 = sbuf("tg3", [128, 1], F32)
    tg4 = sbuf("tg4", [128, 1], F32)
    convb = sbuf("convb", [128, 24], F32)
    bq_s = sbuf("bq_s", [128, 8], F32)
    bk_s = sbuf("bk_s", [128, 8], F32)
    bo_s = sbuf("bo_s", [128, 8], F32)
    projb = sbuf("projb", [128, 32], F32)
    gam = sbuf("gam", [128, 32], F32)
    bet = sbuf("bet", [128, 32], F32)
    bv_row = sbuf("bv_row", [1, W1], F32)
    bcast_tok = sbuf("bcast_tok", [128, NT], F32)
    sm8 = sbuf("sm8", [128, 16], F32)
    bv_bc = bcast_tok
    route_bc = bcast_tok
    route_row = tsc[0][0:1, 0:NT]
    row8 = tsc[1]
    xb_sl = [arB[:, i * NTH:(i + 1) * NTH] for i in range(3)]
    xq_sl = [arB[:, (3 + i) * NTH:(4 + i) * NTH] for i in range(3)]

    B_Q, B_K, B_V = 0, 8192, 16384
    B_QH = 24576
    B_KH = 0
    B_VH = 8192
    B_O = 16384
    B_P = 32768
    B_PT = 33792
    B_QHO = 34816
    B_XLAT = 0
    B_H = 0
    ESC_STG = 16384
    WO_STG = 24576
    SF_STG = 0
    WO_R = 8192
    A_XN = 0
    A_ESC = 0

    xn = lambda k: arA[:, A_XN + k * NTH: A_XN + (k + 1) * NTH]

    s_cl = mksem("s_cl")
    s_xg = [mksem(f"s_xg{i}") for i in range(3)]
    s_xf = [mksem(f"s_xf{i}") for i in range(2)]
    s_w = [mksem(f"s_w{i}") for i in range(2)]
    s_wc = mksem("s_wc")
    s_t = mksem("s_t")
    s_v = mksem("s_v")
    s_s = mksem("s_s")
    s_g2 = mksem("s_g2")
    s_sw = mksem("s_sw")
    s_saw = [mksem(f"s_saw{i}") for i in range(2)]
    s_sfw = [mksem(f"s_sfw{i}") for i in range(2)]
    s_sfr = [mksem(f"s_sfr{i}") for i in range(2)]
    s_esw = [mksem(f"s_esw{i}") for i in range(2)]
    s_wow = [mksem(f"s_wow{i}") for i in range(2)]
    s_wor = mksem("s_wor")
    s_esr = mksem("s_esr")
    s_so = [mksem(f"s_so{i}") for i in range(2)]
    s_d = mksem("s_d")
    s_ar2 = [mksem(f"s_ar2{i}") for i in range(2)]
    s_sfg = [mksem(f"s_sfg{i}") for i in range(2)]
    s_r2 = [mksem(f"s_r2{i}") for i in range(2)]

    C = {}

    def inc(instr, sem, n=1):
        instr.then_inc(sem, n)
        C[id(sem)] = C.get(id(sem), 0) + n
        return instr

    def cnt(sem):
        return C.get(id(sem), 0)

    class WS:
        def __init__(self):
            self.i_dma = 0
            self.i_use = 0
            self.held = 0

        def dma(self, g, src_ap):
            i = self.i_dma
            if i >= 2:
                g.wait_ge(s_wc, i - 1)
            dst = slot[i % 2].ap().rearrange("p (kb c) -> p kb c", c=512)
            inc(g.dma_start(out=dst, in_=src_ap), s_w[i % 2], 16)
            self.i_dma += 1

        def acquire(self, te):
            i = self.i_use + self.held
            assert self.held < 2
            te.wait_ge(s_w[i % 2], 16 * (i // 2 + 1))
            self.held += 1
            return slot[i % 2]

        def release(self, last_mm):
            inc(last_mm, s_wc, 1)
            self.i_use += 1
            self.held -= 1

    ws = WS()

    psum_es = ExitStack()
    gp = []

    def psum_open(n=8, tag=""):
        assert not gp
        for i in range(n):
            gp.append(psum_es.enter_context(nc.psum_tensor(f"gp_{tag}{i}", [128, 512], F32)))

    def psum_close():
        psum_es.pop_all().close()
        gp.clear()

    with nc.Block() as blk:
        @blk.sync
        def _(s):
            for dst, src in [
                (mask1, mask4_h[:, 0:128]), (idb, idb_h[:, :]), (idf, idf_h[:, :]),
                (evsc, evsc_h[:, :]), (odsc, odsc_h[:, :]),
                (evm, evmask_h[:, :]), (odm, odmask_h[:, :]),
                (onesb, onesb_h[:, :]), (onesf, onesf_h[:, :]),
                (convb, convb_h[:, :]),
                (bq_s, bq_h[:, :]),
                (bk_s, bk_h[:, :]),
                (bo_s, bo_h[:, :]),
                (projb, projb_h[:, :]),
                (gam, lng_h[:, :]),
                (bet, lnb_h[:, :]),
                (bv_row, bv_h.rearrange("h e -> (h e)")[None, :]),
            ]:
                inc(s.dma_start(out=dst[:], in_=src), s_cl, 16)
        N_CONST = cnt(s_cl)

        @blk.gpsimd
        def _(g):
            inc(g.dma_start(out=gate_sb[:], in_=gatew_h[:, :]), s_g2, 16)

        def emit_tap(name, srcs):
            if name not in taps:
                return
            waits = ([(s_v, cnt(s_v)), (s_t, cnt(s_t)), (s_s, cnt(s_s))] +
                     [(x, cnt(x)) for x in list(s_saw) + list(s_sfw) + list(s_esw)])

            @blk.sync
            def _(s):
                for sem, val in waits:
                    s.wait_ge(sem, val)
                for dst, src in srcs:
                    inc(s.dma_start(out=dst, in_=src), s_d, 16)

        st_es = ExitStack()
        ps_sum = [st_es.enter_context(nc.psum_tensor(f"pssum{i}", [1, 512], F32)) for i in range(3)]
        ps_sq = [st_es.enter_context(nc.psum_tensor(f"pssq{i}", [1, 512], F32)) for i in range(3)]
        chunks3 = [(0, 512), (512, 1024), (1024, 1026)]
        b_t = cnt(s_t)
        b_v = cnt(s_v)

        @blk.gpsimd
        def _(g):
            for k in range(KC_D):
                sl = k % 3
                if k >= 3:
                    g.wait_ge(s_t, b_t + k - 2)
                inc(g.dma_start(out=xb_sl[sl][:], in_=x_v[k]), s_xg[sl], 16)

        @blk.vector
        def _(v):
            for k in range(KC_D):
                sl = k % 3
                v.wait_ge(s_xg[sl], 16 * (k // 3 + 1))
                if k >= 3:
                    v.wait_ge(s_t, b_t + k - 2)
                inc(v.tensor_tensor(xq_sl[sl][:], xb_sl[sl][:], xb_sl[sl][:], ALU.mult), s_v, 1)

        @blk.tensor
        def _(te):
            te.wait_ge(s_cl, N_CONST)
            for k in range(KC_D):
                sl = k % 3
                te.wait_ge(s_xg[sl], 16 * (k // 3 + 1))
                te.wait_ge(s_v, b_v + k + 1)
                mm = None
                for ci, (c0, c1) in enumerate(chunks3):
                    te.matmul(ps_sum[ci][0:1, 0:c1 - c0], onesb[:], xb_sl[sl][:, c0:c1],
                              start=(k == 0), stop=(k == KC_D - 1))
                    mm = te.matmul(ps_sq[ci][0:1, 0:c1 - c0], onesb[:], xq_sl[sl][:, c0:c1],
                                   start=(k == 0), stop=(k == KC_D - 1))
                inc(mm, s_t, 1)

        def rowreg(which, c0, c1):
            t = {"MN": tsc[0], "VE": tsc[1], "RS": xf_sl[0], "T1": xf_sl[1]}[which]
            return t[0:1, c0:c1]

        def emit_ln_finalize(n, ps_sum_l, ps_sq_l, chunks_l, denom):
            @blk.vector
            def _(v):
                v.wait_ge(s_t, cnt(s_t))
                for ci, (c0, c1) in enumerate(chunks_l):
                    w = c1 - c0
                    v.tensor_scalar(rowreg("MN", c0, c1), ps_sum_l[ci][0:1, 0:w],
                                    -1.0 / denom, None, ALU.mult)
                    v.tensor_scalar(rowreg("VE", c0, c1), ps_sq_l[ci][0:1, 0:w],
                                    1.0 / denom, None, ALU.mult)
                v.tensor_tensor(rowreg("T1", 0, n), rowreg("MN", 0, n),
                                rowreg("MN", 0, n), ALU.mult)
                v.tensor_tensor(rowreg("VE", 0, n), rowreg("VE", 0, n),
                                rowreg("T1", 0, n), ALU.subtract)
                inc(v.tensor_scalar(rowreg("VE", 0, n), rowreg("VE", 0, n),
                                    EPS, None, ALU.add), s_v, 1)

            @blk.scalar
            def _(sc):
                sc.wait_ge(s_v, cnt(s_v))
                inc(sc.activation(rowreg("T1", 0, n), rowreg("VE", 0, n), AF.Sqrt),
                    s_s, 1)

            @blk.vector
            def _(v):
                v.wait_ge(s_s, cnt(s_s))
                inc(v.reciprocal(rowreg("RS", 0, n), rowreg("T1", 0, n)), s_sw, 1)
                v.wait_ge(s_sw, cnt(s_sw))
                v.tensor_tensor(rowreg("T1", 0, n), rowreg("RS", 0, n),
                                rowreg("RS", 0, n), ALU.mult)
                v.tensor_tensor(rowreg("T1", 0, n), rowreg("T1", 0, n),
                                rowreg("VE", 0, n), ALU.mult)
                v.tensor_scalar(rowreg("T1", 0, n), rowreg("T1", 0, n),
                                -0.5, 1.5, ALU.mult, ALU.add)
                inc(v.tensor_tensor(rowreg("RS", 0, n), rowreg("RS", 0, n),
                                    rowreg("T1", 0, n), ALU.mult), s_v, 1)

        emit_ln_finalize(NTH, ps_sum, ps_sq, chunks3, float(D))
        st_es.pop_all().close()

        bc_es = ExitStack()
        bc_ps = [bc_es.enter_context(nc.psum_tensor(f"bcps{i}", [128, 512], F32)) for i in range(2)]

        def emit_bcast(src_off, dst, n):
            b_t3 = cnt(s_t)
            b_v3 = cnt(s_v)
            nch = (n + 511) // 512

            @blk.tensor
            def _(te):
                te.wait_ge(s_v, b_v3)
                for ci in range(nch):
                    if ci >= 2:
                        te.wait_ge(s_v, b_v3 + ci - 1)
                    c0, c1 = ci * 512, min(n, (ci + 1) * 512)
                    inc(te.matmul(bc_ps[ci % 2][:, 0:c1 - c0], onesf[:],
                                  rowreg(src_off, c0, c1), start=True, stop=True), s_t, 1)

            @blk.vector
            def _(v):
                for ci in range(nch):
                    v.wait_ge(s_t, b_t3 + ci + 1)
                    c0, c1 = ci * 512, min(n, (ci + 1) * 512)
                    inc(v.tensor_copy(dst[:, c0:c1], bc_ps[ci % 2][:, 0:c1 - c0]), s_v, 1)

        emit_bcast("MN", mneg_bc, NTH)
        emit_bcast("RS", rstd_bc, NTH)
        bc_es.pop_all().close()

        b_v = cnt(s_v)
        b_xf = [cnt(s_xf[0]), cnt(s_xf[1])]

        @blk.sync
        def _(s):
            s.wait_ge(s_t, cnt(s_t))
            for k in range(KC_D):
                sl = k % 2
                if k >= 2:
                    s.wait_ge(s_v, b_v + k - 1)
                inc(s.dma_start(out=xf_sl[sl][:], in_=x_v[k]), s_xf[sl], 16)

        @blk.vector
        def _(v):
            for k in range(KC_D):
                sl = k % 2
                v.wait_ge(s_xf[sl], b_xf[sl] + 16 * (k // 2 + 1))
                v.tensor_tensor(xf_sl[sl][:], xf_sl[sl][:], mneg_bc[:], ALU.add)
                inc(v.tensor_tensor(xn(k), xf_sl[sl][:], rstd_bc[:], ALU.mult), s_v, 1)

        if "xn" in taps:
            emit_tap("xn", [(tap_h["xn"].rearrange("(mc p) t -> mc p t", p=128)[k], xn(k))
                            for k in range(KC_D)])

        psum_open(8, tag="a")
        conv_batches = [convw_v[w][:, kb * 8:(kb + 1) * 8, mg * 512:(mg + 1) * 512]
                        for mg in range(6) for w in range(3) for kb in range(4)]
        wc0 = cnt(s_wc)
        b_s = cnt(s_s)

        @blk.gpsimd
        def _(g):
            for src in conv_batches:
                ws.dma(g, src)

        @blk.tensor
        def _(te):
            te.wait_ge(s_v, cnt(s_v))
            for mg in range(6):
                if mg >= 1:
                    te.wait_ge(s_s, b_s + 8 * mg)
                last = None
                for w in range(3):
                    for kb in range(4):
                        sl = ws.acquire(te)
                        for kk in range(8):
                            k = kb * 8 + kk
                            for mj in range(4):
                                for nh in range(2):
                                    last = te.matmul(
                                        gp[mj * 2 + nh][:],
                                        sl[:, kk * 512 + mj * 128: kk * 512 + (mj + 1) * 128],
                                        xn(k)[:, w + nh * 512: w + nh * 512 + 512],
                                        start=(w == 0 and k == 0), stop=(w == 2 and k == 31))
                        ws.release(last)

        @blk.scalar
        def _(sc):
            sc.wait_ge(s_cl, N_CONST)
            for mg in range(6):
                sc.wait_ge(s_wc, wc0 + 12 * (mg + 1))
                for mj in range(4):
                    m = mg * 4 + mj
                    off = [B_Q, B_K, B_V][m // 8]
                    for nh in range(2):
                        dst = arB[:, off + (m % 8) * NT + nh * 512:
                                  off + (m % 8) * NT + nh * 512 + 512]
                        inc(sc.activation(dst, gp[mj * 2 + nh][:], AF.Identity,
                                          bias=convb[:, m:m + 1]), s_s, 1)

        if "qkv" in taps:
            emit_tap("qkv", [(tap_h["qkv"].rearrange("(mc p) t -> mc p t", p=128)[m],
                              arB[:, [B_Q, B_K, B_V][m // 8] + (m % 8) * NT:
                                  [B_Q, B_K, B_V][m // 8] + (m % 8 + 1) * NT])
                             for m in range(24)])

        @blk.vector
        def _(v):
            v.wait_ge(s_cl, N_CONST)
            v.tensor_scalar(bq_s[:], bq_s[:], 0.125, None, ALU.mult)
            v.tensor_scalar(bq_e[:], bq_s[:], evm[:, 0:1], None, ALU.mult)
            inc(v.tensor_scalar(bq_o[:], bq_s[:], odm[:, 0:1], None, ALU.mult), s_v, 1)

        def head_proj(w_view, bias_col, scale, src_off, dst_off, tapname,
                      masked=None):
            batches = [w_view[:, 0:8, mg * 512:(mg + 1) * 512] for mg in range(2)]
            wc0 = cnt(s_wc)
            b_s2 = cnt(s_s)
            b_v2 = cnt(s_v)

            @blk.gpsimd
            def _(g):
                for src in batches:
                    ws.dma(g, src)

            @blk.tensor
            def _(te):
                for mg in range(2):
                    te.wait_ge(s_s, b_s2 + 8 * mg)
                    sl = ws.acquire(te)
                    last = None
                    for kk in range(8):
                        for mj in range(4):
                            for nh in range(2):
                                last = te.matmul(
                                    gp[mj * 2 + nh][:],
                                    sl[:, kk * 512 + mj * 128: kk * 512 + (mj + 1) * 128],
                                    arB[:, src_off + kk * NT + nh * 512:
                                        src_off + kk * NT + nh * 512 + 512],
                                    start=(kk == 0), stop=(kk == 7))
                    ws.release(last)

            @blk.scalar
            def _(sc):
                sc.wait_ge(s_v, b_v2)
                for mg in range(2):
                    sc.wait_ge(s_wc, wc0 + mg + 1)
                    for mj in range(4):
                        m = mg * 4 + mj
                        for nh in range(2):
                            if masked is None:
                                dst = arB[:, dst_off + m * NT + nh * 512:
                                          dst_off + m * NT + nh * 512 + 512]
                                inc(sc.activation(dst, gp[mj * 2 + nh][:], AF.Identity,
                                                  bias=bias_col[:, m:m + 1], scale=scale),
                                    s_s, 1)
                            else:
                                de, do = masked
                                dste = arB[:, de + m * NT + nh * 512:
                                           de + m * NT + nh * 512 + 512]
                                dsto = arB[:, do + m * NT + nh * 512:
                                           do + m * NT + nh * 512 + 512]
                                sc.activation(dste, gp[mj * 2 + nh][:], AF.Identity,
                                              bias=bq_e[:, m:m + 1], scale=evsc[:, 0:1])
                                inc(sc.activation(dsto, gp[mj * 2 + nh][:], AF.Identity,
                                                  bias=bq_o[:, m:m + 1], scale=odsc[:, 0:1]),
                                    s_s, 1)

            if tapname in taps:
                emit_tap(tapname, [(tap_h[tapname].rearrange("(mc p) t -> mc p t", p=128)[m],
                                    arB[:, dst_off + m * NT: dst_off + (m + 1) * NT])
                                   for m in range(8)])

        head_proj(wq_v, bq_s, 0.125, B_Q, B_QH, "qh", masked=(B_QH, B_QHO))
        head_proj(wk_v, bk_s, 1.0, B_K, B_KH, "kh")

        @blk.tensor
        def _(te):
            te.wait_ge(s_s, cnt(s_s))
            te.matmul(gp[0][:], onesf[:], bv_row[0:1, 0:512], start=True, stop=True)
            inc(te.matmul(gp[1][:], onesf[:], bv_row[0:1, 512:1024], start=True, stop=True),
                s_t, 1)

        @blk.vector
        def _(v):
            v.wait_ge(s_t, cnt(s_t))
            v.tensor_copy(bv_bc[:, 0:512], gp[0][:])
            inc(v.tensor_copy(bv_bc[:, 512:1024], gp[1][:]), s_v, 1)

        b_t = cnt(s_t)
        b_v = cnt(s_v)

        @blk.gpsimd
        def _(g):
            for nh in range(2):
                ws.dma(g, wv_v[:, 0:8, nh * 512:(nh + 1) * 512])

        @blk.tensor
        def _(te):
            te.wait_ge(s_v, b_v)
            sl0 = ws.acquire(te)
            sl1 = ws.acquire(te)
            last = None
            for b in range(8):
                pr = (b % 2) * 2
                if b >= 2:
                    te.wait_ge(s_v, b_v + (b - 1))
                for nh, sl in ((0, sl0), (1, sl1)):
                    for k in range(8):
                        last = te.matmul(gp[4 + pr + nh][:],
                                         arB[:, B_V + k * NT + b * 128: B_V + k * NT + (b + 1) * 128],
                                         sl[:, k * 512:(k + 1) * 512],
                                         start=(k == 0), stop=(k == 7))
                inc(last, s_t, 1)
            ws.i_use += 2
            ws.held -= 2

        @blk.vector
        def _(v):
            for b in range(8):
                v.wait_ge(s_t, b_t + b + 1)
                pr = (b % 2) * 2
                for nh in range(2):
                    dst = arB[:, B_VH + b * 1024 + nh * 512: B_VH + b * 1024 + (nh + 1) * 512]
                    cp = v.tensor_tensor(dst, gp[4 + pr + nh][:],
                                         bv_bc[:, nh * 512:(nh + 1) * 512], ALU.add)
                    if b == 7:
                        inc(cp, s_wc, 1)
                    elif nh == 1:
                        inc(cp, s_v, 1)
                if b == 7:
                    inc(v.tensor_copy(sm8[:, 15:16], bv_bc[:, 0:1]), s_v, 1)

        if "vh" in taps:
            emit_tap("vh", [(tap_h["vh"].rearrange("(b p) w -> b p w", p=128)[b],
                             arB[:, B_VH + b * 1024: B_VH + (b + 1) * 1024])
                            for b in range(8)])

        psum_close()
        at_es = ExitStack()
        ps_sc = [at_es.enter_context(nc.psum_tensor(f"psc{i}", [128, 512], F32)) for i in range(2)]
        ps_pt = at_es.enter_context(nc.psum_tensor("pspt", [128, 1024], BF16))
        ps_o = at_es.enter_context(nc.psum_tensor("pso", [128, 512], F32))
        ps_tr = [at_es.enter_context(nc.psum_tensor(f"pstr{i}", [128, 128], BF16))
                 for i in range(2)]
        sc_f, e_f = spw[0], spw[1]
        b_t = cnt(s_t)
        b_v = cnt(s_v)
        b_s = cnt(s_s)

        @blk.tensor
        def _(te):
            for wv_i in range(16):
                b, w = wv_i // 2, wv_i % 2
                te.wait_ge(s_v, b_v + 4 * wv_i)
                mm = None
                for j in range(8):
                    h = w * 8 + j
                    qb = B_QH if h % 2 == 0 else B_QHO
                    qs = arB[:, qb + (h // 2) * NT + b * 128:
                             qb + (h // 2) * NT + (b + 1) * 128]
                    ks = arB[:, B_KH + (h // 2) * NT + b * 128:
                             B_KH + (h // 2) * NT + (b + 1) * 128]
                    mm = te.matmul(ps_sc[j // 4][:, (j % 4) * 128:(j % 4 + 1) * 128],
                                   qs, ks, start=(j % 4 == 0), stop=(j % 4 == 3))
                inc(mm, s_t, 1)
                te.wait_ge(s_v, b_v + 4 * wv_i + 2)
                for j in range(8):
                    te.matmul(ps_pt[:, j * 128:(j + 1) * 128],
                              arB[:, B_P + j * 128: B_P + (j + 1) * 128], idb[:],
                              is_transpose=True, start=(j == 0), stop=(j == 7))
                inc(te.matmul(ps_sc[0][0:1, 0:1], onesb[0:1, 0:1], onesb[0:1, 0:1],
                              start=True, stop=True), s_t, 1)
                te.wait_ge(s_v, b_v + 4 * wv_i + 3)
                mm = None
                for j in range(8):
                    h = w * 8 + j
                    vs = arB[:, B_VH + b * 1024 + h * 64: B_VH + b * 1024 + (h + 1) * 64]
                    mm = te.matmul(ps_o[:, j * 64:(j + 1) * 64],
                                   arB[:, B_PT + j * 128: B_PT + (j + 1) * 128], vs,
                                   start=(j == 0), stop=(j == 7))
                inc(mm, s_t, 1)

        @blk.vector
        def _(v):
            for wv_i in range(16):
                b, w = wv_i // 2, wv_i % 2
                v.wait_ge(s_t, b_t + 3 * wv_i + 1)
                v.wait_ge(s_s, b_s + wv_i)
                for j in range(8):
                    v.tensor_tensor(sc_f[:, j * 128:(j + 1) * 128],
                                    ps_sc[j // 4][:, (j % 4) * 128:(j % 4 + 1) * 128],
                                    mask1[:], ALU.add)
                inc(v.reduce_max(sm8[:, 8:9], sc_f[:], axis=AX.X, negate=True), s_v, 1)
                v.wait_ge(s_s, b_s + wv_i + 1)
                inc(v.reduce_sum(sm8[:, 0:8],
                                 e_f[:].rearrange("p (j c) -> p j c", c=128),
                                 axis=AX.X), s_sw, 1)
                v.wait_ge(s_sw, cnt(s_sw))
                inc(v.reciprocal(sm8[:, 8:16], sm8[:, 0:8]), s_sw, 1)
                v.wait_ge(s_sw, cnt(s_sw))
                cp = None
                for j in range(8):
                    cp = v.tensor_scalar(arB[:, B_P + j * 128: B_P + (j + 1) * 128],
                                         e_f[:, j * 128:(j + 1) * 128], sm8[:, 8 + j:9 + j],
                                         None, ALU.mult)
                inc(cp, s_v, 1)
                v.wait_ge(s_t, b_t + 3 * wv_i + 2)
                inc(v.tensor_copy(arB[:, B_PT:B_PT + 1024], ps_pt[:]), s_v, 1)
                v.wait_ge(s_t, b_t + 3 * wv_i + 3)
                inc(v.tensor_copy(arB[:, B_O + b * 1024 + w * 512:
                                      B_O + b * 1024 + (w + 1) * 512], ps_o[:]), s_v, 1)

        @blk.scalar
        def _(sc):
            for wv_i in range(16):
                sc.wait_ge(s_v, b_v + 4 * wv_i + 1)
                inc(sc.activation(e_f[:], sc_f[:], AF.Exp, bias=sm8[:, 8:9]), s_s, 1)

        pairs = [(b, t) for b in range(8) for t in range(b, 8)]
        b_t = cnt(s_t)
        b_v = cnt(s_v)

        @blk.tensor
        def _(te):
            te.wait_ge(s_v, b_v)
            for i, (b, t) in enumerate(pairs):
                if i >= 1:
                    te.wait_ge(s_v, b_v + i)
                te.transpose(ps_tr[0][:], arB[:, B_O + b * 1024 + t * 128:
                                               B_O + b * 1024 + (t + 1) * 128], idb[:])
                if b != t:
                    te.transpose(ps_tr[1][:], arB[:, B_O + t * 1024 + b * 128:
                                                   B_O + t * 1024 + (b + 1) * 128], idb[:])
                inc(te.matmul(ps_sc[0][0:1, 0:1], onesb[0:1, 0:1], onesb[0:1, 0:1],
                              start=True, stop=True), s_t, 1)

        @blk.vector
        def _(v):
            for i, (b, t) in enumerate(pairs):
                v.wait_ge(s_t, b_t + i + 1)
                cp = v.tensor_copy(arB[:, B_O + t * 1024 + b * 128:
                                       B_O + t * 1024 + (b + 1) * 128], ps_tr[0][:])
                if b != t:
                    cp = v.tensor_copy(arB[:, B_O + b * 1024 + t * 128:
                                           B_O + b * 1024 + (t + 1) * 128], ps_tr[1][:])
                inc(cp, s_v, 1)

        if "o" in taps:
            emit_tap("o", [(tap_h["o"].rearrange("(mc p) t -> mc p t", p=128)[m],
                            arB[:, B_O + m * NT: B_O + (m + 1) * NT]) for m in range(8)])

        at_es.pop_all().close()
        psum_open(8, tag="b")
        wc0 = cnt(s_wc)
        b_s = cnt(s_s)

        @blk.gpsimd
        def _(g):
            for mg in range(2):
                ws.dma(g, wo_v[:, 0:8, mg * 512:(mg + 1) * 512])

        @blk.tensor
        def _(te):
            te.wait_ge(s_v, cnt(s_v))
            for mg in range(2):
                if mg >= 1:
                    te.wait_ge(s_s, b_s + 8 * mg)
                sl = ws.acquire(te)
                last = None
                for kk in range(8):
                    for mj in range(4):
                        for nh in range(2):
                            last = te.matmul(gp[mj * 2 + nh][:],
                                             sl[:, kk * 512 + mj * 128: kk * 512 + (mj + 1) * 128],
                                             arB[:, B_O + kk * NT + nh * 512:
                                                 B_O + kk * NT + nh * 512 + 512],
                                             start=(kk == 0), stop=(kk == 7))
                ws.release(last)

        b_wow = [cnt(s_wow[0]), cnt(s_wow[1])]
        b_s_wo = cnt(s_s)

        @blk.scalar
        def _(sc):
            for mg in range(2):
                sc.wait_ge(s_wc, wc0 + mg + 1)
                for mj in range(4):
                    m = mg * 4 + mj
                    if m >= 2:
                        sc.wait_ge(s_wow[m % 2], b_wow[m % 2] + 16 * (m // 2))
                    for nh in range(2):
                        dst = arB[:, WO_STG + (m % 2) * NT + nh * 512:
                                  WO_STG + (m % 2) * NT + nh * 512 + 512]
                        inc(sc.activation(dst, gp[mj * 2 + nh][:], AF.Identity,
                                          bias=bo_s[:, m:m + 1]), s_s, 1)

        @blk.sync
        def _(s):
            for m in range(8):
                s.wait_ge(s_s, b_s_wo + 2 * (m + 1))
                inc(s.dma_start(out=wosp_v[m],
                                in_=arB[:, WO_STG + (m % 2) * NT: WO_STG + (m % 2 + 1) * NT]),
                    s_wow[m % 2], 16)

        if "wo_out" in taps:
            emit_tap("wo_out", [(tap_h["wo_out"][:, :], wosp_h[:, :])])

        b_t = cnt(s_t)
        b_v = cnt(s_v)

        @blk.tensor
        def _(te):
            te.wait_ge(s_g2, 16)
            te.wait_ge(s_s, cnt(s_s))
            for b in range(8):
                if b >= 1:
                    te.wait_ge(s_v, b_v + b)
                mm = None
                for k in range(KC_D):
                    mm = te.matmul(gp[0][:, 0:8],
                                   xn(k)[:, 1 + b * 128: 1 + (b + 1) * 128],
                                   gate_sb[:, k * 8:(k + 1) * 8],
                                   start=(k == 0), stop=(k == KC_D - 1))
                inc(mm, s_t, 1)

        @blk.vector
        def _(v):
            def F(op):
                inc(op, s_sw, 1)
                v.wait_ge(s_sw, cnt(s_sw))
            for b in range(8):
                v.wait_ge(s_t, b_t + b + 1)
                F(v.tensor_copy(gsb[:], gp[0][:, 0:8]))
                F(v.reduce_max(tg2[:], gsb[:], axis=AX.X))
                F(v.tensor_scalar(tg[:], gsb[:], tg2[:, 0:1], None, ALU.is_equal))
                F(v.reduce_sum(tg3[:], tg[:], axis=AX.X))
                F(v.scalar_tensor_tensor(tg[:], tg[:], 1e30, gsb[:], ALU.mult, ALU.subtract))
                F(v.tensor_scalar(tg[:], tg[:], -1.0, None, ALU.mult))
                F(v.reduce_max(tg4[:], tg[:], axis=AX.X))
                F(v.tensor_scalar(tg3[:], tg3[:], 2.0, None, ALU.is_ge))
                F(v.tensor_tensor(route_cols[:, b:b + 1], tg2[:], tg4[:], ALU.subtract))
                F(v.tensor_tensor(route_cols[:, b:b + 1], route_cols[:, b:b + 1], tg3[:],
                                  ALU.mult))
                F(v.tensor_tensor(route_cols[:, b:b + 1], route_cols[:, b:b + 1], tg4[:],
                                  ALU.add))
                inc(v.tensor_tensor(route_cols[:, b:b + 1], route_cols[:, b:b + 1], tg2[:],
                                    ALU.add), s_v, 1)

        @blk.tensor
        def _(te):
            te.wait_ge(s_v, cnt(s_v))
            te.transpose(gp[1][0:8, 0:128], route_cols[:], idf[:])
            inc(te.matmul(gp[2][0:1, 0:1], onesb[0:1, 0:1], onesb[0:1, 0:1],
                          start=True, stop=True), s_t, 1)

        @blk.vector
        def _(v):
            v.wait_ge(s_t, cnt(s_t))
            inc(v.tensor_copy(row8[0:8, 0:128], gp[1][0:8, 0:128]), s_v, 1)

        b_g2 = cnt(s_g2)

        @blk.gpsimd
        def _(g):
            g.wait_ge(s_v, cnt(s_v))
            for j in range(8):
                inc(g.dma_start(out=route_row[0:1, j * 128:(j + 1) * 128],
                                in_=row8[j:j + 1, 0:128]), s_g2, 16)

        @blk.tensor
        def _(te):
            te.wait_ge(s_g2, cnt(s_g2))
            te.matmul(gp[2][:], onesf[:], route_row[0:1, 0:512], start=True, stop=True)
            inc(te.matmul(gp[3][:], onesf[:], route_row[0:1, 512:1024], start=True, stop=True),
                s_t, 1)

        @blk.vector
        def _(v):
            v.wait_ge(s_t, cnt(s_t))
            v.tensor_copy(route_bc[:, 0:512], gp[2][:])
            inc(v.tensor_copy(route_bc[:, 512:1024], gp[3][:]), s_v, 1)

        if "route" in taps:
            emit_tap("route", [(tap_h["route"][:, :], route_bc[:])])

        wc0 = cnt(s_wc)
        b_v = cnt(s_v)

        @blk.gpsimd
        def _(g):
            for mg in range(4):
                for kb in range(4):
                    ws.dma(g, fc1w_v[:, kb * 8:(kb + 1) * 8, mg * 512:(mg + 1) * 512])

        @blk.tensor
        def _(te):
            te.wait_ge(s_v, b_v)
            for mg in range(4):
                if mg >= 1:
                    te.wait_ge(s_v, b_v + 8 * mg)
                last = None
                for kb in range(4):
                    sl = ws.acquire(te)
                    for kk in range(8):
                        k = kb * 8 + kk
                        for mj in range(4):
                            for nh in range(2):
                                last = te.matmul(
                                    gp[mj * 2 + nh][:],
                                    sl[:, kk * 512 + mj * 128: kk * 512 + (mj + 1) * 128],
                                    xn(k)[:, 1 + nh * 512: 1 + nh * 512 + 512],
                                    start=(k == 0), stop=(k == KC_D - 1))
                    ws.release(last)

        @blk.vector
        def _(v):
            for mg in range(4):
                v.wait_ge(s_wc, wc0 + 4 * (mg + 1))
                for mj in range(4):
                    m = mg * 4 + mj
                    for nh in range(2):
                        dst = arB[:, B_XLAT + m * NT + nh * 512: B_XLAT + m * NT + nh * 512 + 512]
                        inc(v.tensor_copy(dst, gp[mj * 2 + nh][:]), s_v, 1)

        if "xlat" in taps:
            emit_tap("xlat", [(tap_h["xlat"].rearrange("(mc p) t -> mc p t", p=128)[m],
                               arB[:, B_XLAT + m * NT: B_XLAT + (m + 1) * NT])
                              for m in range(16)])

        wc0 = cnt(s_wc)
        b_v = cnt(s_v)
        b_esw = [cnt(s_esw[0]), cnt(s_esw[1])]
        esc_stage = arB[:, ESC_STG:ESC_STG + 2 * NT]

        @blk.gpsimd
        def _(g):
            for mg in range(4):
                for kb in range(2):
                    ws.dma(g, expw_v[:, kb * 8:(kb + 1) * 8, mg * 512:(mg + 1) * 512])

        @blk.tensor
        def _(te):
            te.wait_ge(s_v, b_v)
            for mg in range(4):
                if mg >= 1:
                    te.wait_ge(s_v, b_v + 8 * mg)
                last = None
                for kb in range(2):
                    sl = ws.acquire(te)
                    for kk in range(8):
                        k = kb * 8 + kk
                        for mj in range(4):
                            for nh in range(2):
                                last = te.matmul(
                                    gp[mj * 2 + nh][:],
                                    sl[:, kk * 512 + mj * 128: kk * 512 + (mj + 1) * 128],
                                    arB[:, B_XLAT + k * NT + nh * 512:
                                        B_XLAT + k * NT + nh * 512 + 512],
                                    start=(k == 0), stop=(k == 15))
                    ws.release(last)

        @blk.vector
        def _(v):
            for mg in range(4):
                v.wait_ge(s_wc, wc0 + 2 * (mg + 1))
                for mj in range(4):
                    m = mg * 4 + mj
                    if m >= 2:
                        v.wait_ge(s_esw[m % 2], b_esw[m % 2] + 16 * (m // 2))
                    cp = None
                    for nh in range(2):
                        dst = esc_stage[:, (m % 2) * NT + nh * 512: (m % 2) * NT + nh * 512 + 512]
                        cp = v.tensor_tensor(dst, gp[mj * 2 + nh][:],
                                             route_bc[:, nh * 512:(nh + 1) * 512], ALU.mult)
                    inc(cp, s_v, 2)

        b_v_e = b_v

        @blk.sync
        def _(s):
            for m in range(16):
                s.wait_ge(s_v, b_v_e + 2 * (m + 1))
                inc(s.dma_start(out=esc_v[:, m, :],
                                in_=esc_stage[:, (m % 2) * NT:(m % 2 + 1) * NT]), s_esw[m % 2], 16)

        if "esc" in taps:
            emit_tap("esc", [(tap_h["esc"][:, :], esc_h[:, :])])

        def up_half(half):
            wc0 = cnt(s_wc)
            b_s2 = cnt(s_s)
            b_v2 = cnt(s_v)

            @blk.gpsimd
            def _(g):
                for mg in range(8):
                    for kb in range(4):
                        ws.dma(g, upw_v[:, kb * 8:(kb + 1) * 8,
                                        half * 4096 + mg * 512: half * 4096 + (mg + 1) * 512])

            @blk.tensor
            def _(te):
                te.wait_ge(s_v, b_v2)
                for mg in range(8):
                    if mg >= 1:
                        te.wait_ge(s_s, b_s2 + 8 * mg)
                    last = None
                    for kb in range(4):
                        sl = ws.acquire(te)
                        for kk in range(8):
                            k = kb * 8 + kk
                            for mj in range(4):
                                for nh in range(2):
                                    last = te.matmul(
                                        gp[mj * 2 + nh][:],
                                        sl[:, kk * 512 + mj * 128: kk * 512 + (mj + 1) * 128],
                                        xn(k)[:, 1 + nh * 512: 1 + nh * 512 + 512],
                                        start=(k == 0), stop=(k == KC_D - 1))
                        ws.release(last)

            @blk.scalar
            def _(sc):
                for mg in range(8):
                    sc.wait_ge(s_wc, wc0 + 4 * (mg + 1))
                    for mj in range(4):
                        m = mg * 4 + mj
                        if m >= 2:
                            sc.wait_ge(s_v, b_v2 + 2 * (m - 1))
                        for nh in range(2):
                            inc(sc.activation(tsc[m % 2][:, nh * 512:(nh + 1) * 512],
                                              gp[mj * 2 + nh][:], AF.Relu), s_s, 1)

            @blk.vector
            def _(v):
                for m in range(32):
                    v.wait_ge(s_s, b_s2 + 2 * (m + 1))
                    cp = None
                    for nh in range(2):
                        src = tsc[m % 2][:, nh * 512:(nh + 1) * 512]
                        dst = arB[:, B_H + m * NT + nh * 512: B_H + m * NT + nh * 512 + 512]
                        cp = v.tensor_tensor(dst, src, src, ALU.mult)
                    inc(cp, s_v, 2)

        def down_pass(half, add_fc2):
            wc0 = cnt(s_wc)
            bpg = 6 if add_fc2 else 4
            b_v2 = cnt(s_v)
            wsem = s_sfw if add_fc2 else s_saw
            b_w2 = [cnt(wsem[0]), cnt(wsem[1])]
            tgt_v = sf_v if add_fc2 else sa_v
            b_xfl = [cnt(s_xf[0]), cnt(s_xf[1])]
            n_saw_total = [cnt(s_saw[0]), cnt(s_saw[1])]

            @blk.gpsimd
            def _(g):
                for mg in range(8):
                    for kb in range(4):
                        ws.dma(g, downw_v[:, half * 32 + kb * 8: half * 32 + (kb + 1) * 8,
                                          mg * 512:(mg + 1) * 512])
                    if add_fc2:
                        for kb in range(2):
                            ws.dma(g, fc2w_v[:, kb * 8:(kb + 1) * 8, mg * 512:(mg + 1) * 512])

            if add_fc2:
                @blk.scalar
                def _(sc):
                    sc.wait_ge(s_saw[0], n_saw_total[0])
                    sc.wait_ge(s_saw[1], n_saw_total[1])
                    for m in range(32):
                        sl = m % 2
                        if m >= 2:
                            sc.wait_ge(s_v, b_v2 + m - 1)
                        inc(sc.dma_start(out=xf_sl[sl][:, 0:NT], in_=sa_v[m]),
                            s_ar2[sl], 16)

                @blk.sync
                def _(s):
                    for m in range(32):
                        s.wait_ge(s_v, b_v2 + m + 1)
                        inc(s.dma_start(out=tgt_v[m], in_=spw[m % 2][:]), wsem[m % 2], 16)

            b_s3 = cnt(s_s)

            @blk.tensor
            def _(te):
                te.wait_ge(s_s, b_s3)
                for mg in range(8):
                    if mg >= 1:
                        te.wait_ge(s_v, b_v2 + 4 * mg)
                    last = None
                    for kb in range(4):
                        sl = ws.acquire(te)
                        for kk in range(8):
                            k = kb * 8 + kk
                            for mj in range(4):
                                for nh in range(2):
                                    last = te.matmul(
                                        gp[mj * 2 + nh][:],
                                        sl[:, kk * 512 + mj * 128: kk * 512 + (mj + 1) * 128],
                                        arB[:, B_H + k * NT + nh * 512:
                                            B_H + k * NT + nh * 512 + 512],
                                        start=(k == 0), stop=(not add_fc2 and k == KC_D - 1))
                        ws.release(last)
                    if add_fc2:
                        if mg == 0:
                            te.wait_ge(s_esr, 32)
                        for kb in range(2):
                            sl = ws.acquire(te)
                            for kk in range(8):
                                k = kb * 8 + kk
                                for mj in range(4):
                                    for nh in range(2):
                                        last = te.matmul(
                                            gp[mj * 2 + nh][:],
                                            sl[:, kk * 512 + mj * 128: kk * 512 + (mj + 1) * 128],
                                            arA[:, A_ESC + k * NT + nh * 512:
                                                A_ESC + k * NT + nh * 512 + 512],
                                            start=False, stop=(k == 15))
                            ws.release(last)

            @blk.vector
            def _(v):
                for mg in range(8):
                    v.wait_ge(s_wc, wc0 + bpg * (mg + 1))
                    for mj in range(4):
                        m = mg * 4 + mj
                        if m >= 2:
                            v.wait_ge(wsem[m % 2], b_w2[m % 2] + 16 * (m // 2))
                        if add_fc2:
                            v.wait_ge(s_ar2[m % 2], 16 * (m // 2 + 1))
                        ins = None
                        for nh in range(2):
                            dst = spw[m % 2][:, nh * 512:(nh + 1) * 512]
                            if add_fc2:
                                ins = v.tensor_tensor(dst, gp[mj * 2 + nh][:],
                                                      xf_sl[m % 2][:, nh * 512:(nh + 1) * 512],
                                                      ALU.add)
                            else:
                                ins = v.tensor_copy(dst, gp[mj * 2 + nh][:])
                        inc(ins, s_v, 1)

            if not add_fc2:
                @blk.sync
                def _(s):
                    for m in range(32):
                        s.wait_ge(s_v, b_v2 + m + 1)
                        inc(s.dma_start(out=tgt_v[m], in_=spw[m % 2][:]), wsem[m % 2], 16)

        up_half(0)
        down_pass(0, add_fc2=False)
        if "sa" in taps:
            emit_tap("sa", [(tap_h["sa"][:, :], sa_h[:, :])])
        if "h" in taps:
            emit_tap("h", [(tap_h["h"].rearrange("(mc p) t -> mc p t", p=128)[m],
                            arB[:, B_H + m * NT: B_H + (m + 1) * NT]) for m in range(32)])
        up_half(1)

        b_esr_wc = cnt(s_wc)

        @blk.sync
        def _(s):
            s.wait_ge(s_wc, b_esr_wc)
            s.wait_ge(s_esw[0], cnt(s_esw[0]))
            s.wait_ge(s_esw[1], cnt(s_esw[1]))
            for half in range(2):
                inc(s.dma_start(
                    out=arA[:, 0:16 * NT].rearrange("p (mc t) -> p mc t", t=NT)[
                        :, half * 8:(half + 1) * 8, :],
                    in_=esc_v[:, half * 8:(half + 1) * 8, :]), s_esr, 16)

        down_pass(1, add_fc2=True)
        if "sf" in taps:
            emit_tap("sf", [(tap_h["sf"][:, :], sf_h[:, :])])

        psum_close()
        st2_es = ExitStack()
        ps_sum2 = [st2_es.enter_context(nc.psum_tensor(f"ps2sum{i}", [1, 512], F32)) for i in range(2)]
        ps_sq2 = [st2_es.enter_context(nc.psum_tensor(f"ps2sq{i}", [1, 512], F32)) for i in range(2)]
        chunks2 = [(0, 512), (512, 1024)]
        b_t = cnt(s_t)
        b_v = cnt(s_v)
        b_sfw_tot = [cnt(s_sfw[0]), cnt(s_sfw[1])]
        b_sfr = [cnt(s_sfr[0]), cnt(s_sfr[1])]

        @blk.gpsimd
        def _(g):
            for m in range(KC_D):
                sl = m % 2
                if m >= 2:
                    g.wait_ge(s_t, b_t + m - 1)
                g.wait_ge(s_sfw[sl], b_sfw_tot[sl] - 256 + 16 * (m // 2 + 1))
                inc(g.dma_start(out=arB[:, SF_STG + sl * NT: SF_STG + (sl + 1) * NT],
                                in_=sf_v[m]), s_sfg[sl], 16)

        @blk.vector
        def _(v):
            for m in range(KC_D):
                sl = m % 2
                v.wait_ge(s_sfg[sl], 16 * (m // 2 + 1))
                if m >= 2:
                    v.wait_ge(s_t, b_t + m - 1)
                inc(v.tensor_tensor(arB[:, SF_STG + (2 + sl) * NT: SF_STG + (3 + sl) * NT],
                                    arB[:, SF_STG + sl * NT: SF_STG + (sl + 1) * NT],
                                    arB[:, SF_STG + sl * NT: SF_STG + (sl + 1) * NT],
                                    ALU.mult), s_v, 1)

        @blk.tensor
        def _(te):
            for m in range(KC_D):
                te.wait_ge(s_v, b_v + m + 1)
                mm = None
                for ci, (c0, c1) in enumerate(chunks2):
                    te.matmul(ps_sum2[ci][0:1, :], onesb[:],
                              arB[:, SF_STG + (m % 2) * NT + c0: SF_STG + (m % 2) * NT + c1],
                              start=(m == 0), stop=(m == KC_D - 1))
                    mm = te.matmul(ps_sq2[ci][0:1, :], onesb[:],
                                   arB[:, SF_STG + (2 + m % 2) * NT + c0:
                                       SF_STG + (2 + m % 2) * NT + c1],
                                   start=(m == 0), stop=(m == KC_D - 1))
                inc(mm, s_t, 1)

        emit_ln_finalize(NT, ps_sum2, ps_sq2, chunks2, float(D))
        st2_es.pop_all().close()
        bc_es2 = ExitStack()
        bc_ps = [bc_es2.enter_context(nc.psum_tensor(f"bc2ps{i}", [128, 512], F32)) for i in range(2)]
        emit_bcast("MN", mneg_bc, NT)
        emit_bcast("RS", rstd_bc, NT)
        bc_es2.pop_all().close()

        psum_open(8, tag="c")
        wc0 = cnt(s_wc)
        b_v = cnt(s_v)
        b_s = cnt(s_s)
        b_so = [cnt(s_so[0]), cnt(s_so[1])]
        b_xfl = [cnt(s_xf[0]), cnt(s_xf[1])]
        b_sfr = [cnt(s_sfr[0]), cnt(s_sfr[1])]

        @blk.gpsimd
        def _(g):
            for mg in range(8):
                ws.dma(g, projw_v[:, 0:8, mg * 512:(mg + 1) * 512])

        @blk.sync
        def _(s):
            s.wait_ge(s_t, cnt(s_t))
            s.wait_ge(s_wow[0], cnt(s_wow[0]))
            s.wait_ge(s_wow[1], cnt(s_wow[1]))
            inc(s.dma_start(out=arB[:, WO_R:WO_R + 8 * NT].rearrange(
                    "p (mc t) -> p mc t", t=NT), in_=wosp_r), s_wor, 16)
            for m in range(KC_D):
                s.wait_ge(s_v, b_v + 2 * m + 2)
                inc(s.dma_start(out=out_v[m], in_=spw[m % 2][:]), s_so[m % 2], 16)
            s.wait_ge(s_so[0], b_so[0] + 16 * (KC_D // 2))
            s.wait_ge(s_so[1], b_so[1] + 16 * (KC_D // 2))

        @blk.tensor
        def _(te):
            te.wait_ge(s_v, b_v)
            te.wait_ge(s_wor, cnt(s_wor))
            for mg in range(8):
                if mg >= 1:
                    te.wait_ge(s_v, b_v + 8 * mg)
                sl = ws.acquire(te)
                last = None
                for kk in range(8):
                    for mj in range(4):
                        for nh in range(2):
                            last = te.matmul(gp[mj * 2 + nh][:],
                                             sl[:, kk * 512 + mj * 128: kk * 512 + (mj + 1) * 128],
                                             arB[:, WO_R + kk * NT + nh * 512:
                                                 WO_R + kk * NT + nh * 512 + 512],
                                             start=(kk == 0), stop=(kk == 7))
                ws.release(last)

        @blk.vector
        def _(v):
            for mg in range(8):
                v.wait_ge(s_wc, wc0 + mg + 1)
                for mj in range(4):
                    m = mg * 4 + mj
                    sl = m % 2
                    v.wait_ge(s_r2[sl], 32 * (m // 2 + 1))
                    ins = None
                    for nh in range(2):
                        cs = slice(nh * 512, (nh + 1) * 512)
                        v.tensor_tensor(tsc[0][:, cs], spw[sl][:, cs], mneg_bc[:, cs], ALU.add)
                        ins = v.tensor_tensor(tsc[0][:, cs], tsc[0][:, cs], rstd_bc[:, cs],
                                              ALU.mult)
                    inc(ins, s_v, 1)
                    v.wait_ge(s_s, b_s + m + 1)
                    ins = None
                    for nh in range(2):
                        cs = slice(nh * 512, (nh + 1) * 512)
                        v.tensor_tensor(tsc[1][:, cs], tsc[1][:, cs], xf_sl[sl][:, cs], ALU.add)
                        ins = v.scalar_tensor_tensor(spw[sl][:, cs], gp[mj * 2 + nh][:],
                                                     projb[:, m:m + 1], tsc[1][:, cs],
                                                     ALU.add, ALU.add)
                    inc(ins, s_v, 1)

        @blk.scalar
        def _(sc):
            for m in range(KC_D + 1):
                if m < KC_D:
                    sl = m % 2
                    if m >= 2:
                        sc.wait_ge(s_v, b_v + 2 * m - 2)
                        sc.wait_ge(s_so[sl], b_so[sl] + 16 * (m // 2))
                    else:
                        sc.wait_ge(s_v, b_v)
                    inc(sc.dma_start(out=xf_sl[sl][:, 0:NT], in_=x_v[m][:, 1:1 + NT]),
                        s_r2[sl], 16)
                    inc(sc.dma_start(out=spw[sl][:], in_=sf_v[m]), s_r2[sl], 16)
                if m >= 1:
                    mm = m - 1
                    sc.wait_ge(s_v, b_v + 2 * mm + 1)
                    inc(sc.activation(tsc[1][:, 0:NT], tsc[0][:, 0:NT], AF.Identity,
                                      bias=bet[:, mm:mm + 1], scale=gam[:, mm:mm + 1]),
                        s_s, 1)

    root.close()
    return nc, tap_h


def _prep_inputs(L0_attn_inputs):
    x = np.asarray(L0_attn_inputs)
    shards = []
    for c in range(N_CORES):
        b, t0 = c // 4, (c % 4) * NT
        pad = np.zeros((NTH, D), dtype=np.float32)
        lo, hi = t0 - 1, t0 + NT + 1
        src_lo, src_hi = max(lo, 0), min(hi, 4096)
        pad[src_lo - lo: src_lo - lo + (src_hi - src_lo)] = x[b, src_lo:src_hi]
        shards.append(np.ascontiguousarray(pad.T))
    return shards


def kernel(**inputs):
    nc, _ = build_program()
    xs = _prep_inputs(inputs["L0_attn_inputs"])
    def pm(a, k):
        return np.ascontiguousarray(np.asarray(a, np.float32).reshape(k, 128).T)

    base = {
        "conv_w": np.ascontiguousarray(inputs["L0_attn_conv_w"], dtype=np.float32),
        "conv_b_pm": pm(inputs["L0_attn_conv_b"], 24),
        "wq": np.ascontiguousarray(inputs["L0_attn_wq"], dtype=np.float32),
        "bq_pm": pm(np.asarray(inputs["L0_attn_bq"]).reshape(-1), 8),
        "wk": np.ascontiguousarray(inputs["L0_attn_wk"], dtype=np.float32),
        "bk_pm": pm(np.asarray(inputs["L0_attn_bk"]).reshape(-1), 8),
        "wv": np.ascontiguousarray(inputs["L0_attn_wv"], dtype=np.float32),
        "bv": np.ascontiguousarray(inputs["L0_attn_bv"], dtype=np.float32),
        "wo": np.ascontiguousarray(inputs["L0_attn_wo"], dtype=np.float32),
        "bo_pm": pm(inputs["L0_attn_bo"], 8),
        "proj_w": np.ascontiguousarray(inputs["L0_attn_proj_w"], dtype=np.float32),
        "proj_b_pm": pm(inputs["L0_attn_proj_b"], 32),
        "gate_w_pm": np.ascontiguousarray(
            np.asarray(inputs["L0_ffn_gate_w"], np.float32).reshape(32, 128, 8)
            .transpose(1, 0, 2).reshape(128, 256)),
        "up_w": np.ascontiguousarray(inputs["L0_ffn_up_w"], dtype=np.float32),
        "down_w": np.ascontiguousarray(inputs["L0_ffn_down_w"], dtype=np.float32),
        "fc1_w": np.ascontiguousarray(inputs["L0_ffn_fc1_w"], dtype=np.float32),
        "fc2_w": np.ascontiguousarray(inputs["L0_ffn_fc2_w"], dtype=np.float32),
        "expert_w": np.ascontiguousarray(inputs["L0_ffn_expert_w"], dtype=np.float32),
        "ln_gamma_pm": pm(inputs["L0_ffn_ln_gamma"], 32),
        "ln_beta_pm": pm(inputs["L0_ffn_ln_beta"], 32),
    }
    in_maps = [dict(base, x_fm=xs[c]) for c in range(N_CORES)]
    res = run_bass_kernel_spmd(nc, in_maps, core_ids=list(range(N_CORES)))
    out = np.empty((2, 4096, 4096), dtype=np.float32)
    for c in range(N_CORES):
        b, t0 = c // 4, (c % 4) * NT
        out[b, t0:t0 + NT, :] = res.results[c]["out_fm"].T
    return out


# version 07
# speedup vs baseline: 1.0318x; 1.0318x over previous
import numpy as np
import ml_dtypes
from contextlib import ExitStack
import concourse.bass as bass
import concourse.mybir as mybir
from concourse.bass_utils import run_bass_kernel_spmd

F32 = mybir.dt.float32
BF16 = mybir.dt.bfloat16
AF = mybir.ActivationFunctionType
ALU = mybir.AluOpType
AX = mybir.AxisListType

D = 4096
NT = 1024
NTH = 1026
KC_D = 32
W1 = 1024
NHD = 16
DH = 64
INT = 8192
LAT = 2048
NEG = -1.0e9
EPS = 1e-5
N_CORES = 8


def build_program(taps=()):
    nc = bass.Bass("TRN2", target_bir_lowering=False, detect_race_conditions=False)

    x_h = nc.dram_tensor("x_fm", [D, NTH], F32, kind="ExternalInput").ap()
    convw_h = nc.dram_tensor("conv_w", [3, D, 3 * W1], F32, kind="ExternalInput").ap()
    convb_h = nc.dram_tensor("conv_b_pm", [128, 24], F32, kind="ExternalInput").ap()
    wq_h = nc.dram_tensor("wq", [W1, NHD, DH], F32, kind="ExternalInput").ap()
    bq_h = nc.dram_tensor("bq_pm", [128, 8], F32, kind="ExternalInput").ap()
    wk_h = nc.dram_tensor("wk", [W1, NHD, DH], F32, kind="ExternalInput").ap()
    bk_h = nc.dram_tensor("bk_pm", [128, 8], F32, kind="ExternalInput").ap()
    wv_h = nc.dram_tensor("wv", [W1, NHD, DH], F32, kind="ExternalInput").ap()
    bv_h = nc.dram_tensor("bv", [NHD, DH], F32, kind="ExternalInput").ap()
    wo_h = nc.dram_tensor("wo", [NHD, DH, W1], F32, kind="ExternalInput").ap()
    bo_h = nc.dram_tensor("bo_pm", [128, 8], F32, kind="ExternalInput").ap()
    projw_h = nc.dram_tensor("proj_w", [W1, D], F32, kind="ExternalInput").ap()
    projb_h = nc.dram_tensor("proj_b_pm", [128, 32], F32, kind="ExternalInput").ap()
    gatew_h = nc.dram_tensor("gate_w_pm", [128, 256], F32, kind="ExternalInput").ap()
    upw_h = nc.dram_tensor("up_w", [D, INT], F32, kind="ExternalInput").ap()
    downw_h = nc.dram_tensor("down_w", [INT, D], F32, kind="ExternalInput").ap()
    fc1w_h = nc.dram_tensor("fc1_w", [D, LAT], F32, kind="ExternalInput").ap()
    fc2w_h = nc.dram_tensor("fc2_w", [LAT, D], F32, kind="ExternalInput").ap()
    expw_h = nc.dram_tensor("expert_w", [LAT, LAT], F32, kind="ExternalInput").ap()
    lng_h = nc.dram_tensor("ln_gamma_pm", [128, 32], F32, kind="ExternalInput").ap()
    lnb_h = nc.dram_tensor("ln_beta_pm", [128, 32], F32, kind="ExternalInput").ap()

    out_h = nc.dram_tensor("out_fm", [D, NT], F32, kind="ExternalOutput").ap()

    sa_h = nc.dram_tensor("scr_sa", [D, NT], F32, kind="Internal").ap()
    sf_h = nc.dram_tensor("scr_sf", [D, NT], F32, kind="Internal").ap()
    esc_h = nc.dram_tensor("scr_esc", [LAT, NT], BF16, kind="Internal").ap()
    wosp_h = nc.dram_tensor("scr_wo", [W1, NT], BF16, kind="Internal").ap()

    taps = set(taps)
    tap_h = {}

    def tap_decl(name, shape, dt=F32):
        if name in taps:
            tap_h[name] = nc.dram_tensor("tap_" + name, list(shape), dt,
                                         kind="ExternalOutput").ap()

    tap_decl("xn", [D, NTH], BF16)
    tap_decl("qkv", [3 * W1, NT], BF16)
    tap_decl("qh", [W1, NT], BF16)
    tap_decl("kh", [W1, NT], BF16)
    tap_decl("vh", [8 * 128, W1], BF16)
    tap_decl("o", [W1, NT], BF16)
    tap_decl("wo_out", [W1, NT], BF16)
    tap_decl("route", [128, NT])
    tap_decl("xlat", [LAT, NT], BF16)
    tap_decl("esc", [LAT, NT], BF16)
    tap_decl("h", [INT // 2, NT], BF16)
    tap_decl("sa", [D, NT])
    tap_decl("sf", [D, NT])

    mtri = np.zeros((128, 128), np.float32)
    mtri[np.triu_indices(128, 1)] = NEG
    mask4_h = nc.inline_tensor(np.tile(mtri, (1, 4)).astype(np.float32), name="mask4_c").ap()
    idb_h = nc.inline_tensor(np.eye(128, dtype=ml_dtypes.bfloat16), name="ident_bf").ap()
    idf_h = nc.inline_tensor(np.eye(128, dtype=np.float32), name="ident_f32").ap()
    onesb_h = nc.inline_tensor(np.ones((128, 1), ml_dtypes.bfloat16), name="ones_col").ap()
    _ev = np.zeros((128, 1), np.float32); _ev[0:64] = 1.0
    evmask_h = nc.inline_tensor(_ev, name="evmask_c").ap()
    odmask_h = nc.inline_tensor(1.0 - _ev, name="odmask_c").ap()
    evsc_h = nc.inline_tensor((_ev * 0.125).astype(np.float32), name="evsc_c").ap()
    odsc_h = nc.inline_tensor(((1.0 - _ev) * 0.125).astype(np.float32), name="odsc_c").ap()
    onesf_h = nc.inline_tensor(np.ones((1, 128), np.float32), name="ones_row").ap()

    convw_v = [convw_h[w].rearrange("(kc p) m -> p kc m", p=128) for w in range(3)]
    wq_v = wq_h.rearrange("(kc p) h e -> p kc (h e)", p=128)
    wk_v = wk_h.rearrange("(kc p) h e -> p kc (h e)", p=128)
    wv_v = wv_h.rearrange("(kc p) h e -> p kc (h e)", p=128)
    wo_v = wo_h.rearrange("(kc h2) e w -> (h2 e) kc w", h2=2)
    projw_v = projw_h.rearrange("(kc p) m -> p kc m", p=128)
    upw_v = upw_h.rearrange("(kc p) m -> p kc m", p=128)
    downw_v = downw_h.rearrange("(kc p) m -> p kc m", p=128)
    fc1w_v = fc1w_h.rearrange("(kc p) m -> p kc m", p=128)
    fc2w_v = fc2w_h.rearrange("(kc p) m -> p kc m", p=128)
    expw_v = expw_h.rearrange("(kc p) m -> p kc m", p=128)

    x_v = x_h.rearrange("(mc p) t -> mc p t", p=128)
    sa_v = sa_h.rearrange("(mc p) t -> mc p t", p=128)
    sf_v = sf_h.rearrange("(mc p) t -> mc p t", p=128)
    esc_v = esc_h.rearrange("(mc p) t -> p mc t", p=128)
    wosp_v = wosp_h.rearrange("(mc p) t -> mc p t", p=128)
    wosp_r = wosp_h.rearrange("(mc p) t -> p mc t", p=128)
    out_v = out_h.rearrange("(mc p) t -> mc p t", p=128)

    root = ExitStack()
    sbuf = lambda name, shape, dt: root.enter_context(nc.sbuf_tensor(name, shape, dt))
    mksem = lambda name: root.enter_context(nc.semaphore(name))

    arA = sbuf("arA", [128, KC_D * NTH], BF16)
    arB = sbuf("arB", [128, 43008], BF16)
    slot = [sbuf(f"wslot{i}", [128, 8 * 512], BF16) for i in range(2)]
    gate_sb = sbuf("gate_sb", [128, 256], BF16)
    idb = sbuf("idb", [128, 128], BF16)
    onesb = sbuf("onesb", [128, 1], BF16)
    xf_sl = [sbuf(f"xf{i}", [128, NTH], F32) for i in range(2)]
    spw = [sbuf(f"spw{i}", [128, NT], F32) for i in range(2)]
    tsc = [sbuf(f"tsc{i}", [128, NTH], F32) for i in range(2)]
    mneg_bc = sbuf("mneg_bc", [128, NTH], F32)
    rstd_bc = sbuf("rstd_bc", [128, NTH], F32)
    mask1 = sbuf("mask1", [128, 128], F32)
    idf = sbuf("idf", [128, 128], F32)
    onesf = sbuf("onesf", [1, 128], F32)
    evsc = sbuf("evsc", [128, 1], F32)
    odsc = sbuf("odsc", [128, 1], F32)
    evm = sbuf("evm", [128, 1], F32)
    odm = sbuf("odm", [128, 1], F32)
    bq_e = sbuf("bq_e", [128, 8], F32)
    bq_o = sbuf("bq_o", [128, 8], F32)
    route_cols = sbuf("route_cols", [128, 8], F32)
    gsb = sbuf("gsb", [128, 8], F32)
    tg = sbuf("tg", [128, 8], F32)
    tg2 = sbuf("tg2", [128, 1], F32)
    tg3 = sbuf("tg3", [128, 1], F32)
    tg4 = sbuf("tg4", [128, 1], F32)
    convb = sbuf("convb", [128, 24], F32)
    bq_s = sbuf("bq_s", [128, 8], F32)
    bk_s = sbuf("bk_s", [128, 8], F32)
    bo_s = sbuf("bo_s", [128, 8], F32)
    projb = sbuf("projb", [128, 32], F32)
    gam = sbuf("gam", [128, 32], F32)
    bet = sbuf("bet", [128, 32], F32)
    bv_row = sbuf("bv_row", [1, W1], F32)
    bcast_tok = sbuf("bcast_tok", [128, NT], F32)
    sm8 = sbuf("sm8", [128, 16], F32)
    bv_bc = bcast_tok
    route_bc = bcast_tok
    route_row = tsc[0][0:1, 0:NT]
    row8 = tsc[1]
    xb_sl = [arB[:, i * NTH:(i + 1) * NTH] for i in range(3)]
    xq_sl = [arB[:, (3 + i) * NTH:(4 + i) * NTH] for i in range(3)]

    B_Q, B_K, B_V = 0, 8192, 16384
    B_QH = 24576
    B_KH = 0
    B_VH = 8192
    B_O = 16384
    B_P = 32768
    B_PT = 33792
    B_QHO = 34816
    B_XLAT = 0
    B_H = 0
    ESC_STG = 16384
    WO_STG = 24576
    SF_STG = 0
    WO_R = 8192
    A_XN = 0
    A_ESC = 0

    xn = lambda k: arA[:, A_XN + k * NTH: A_XN + (k + 1) * NTH]

    s_cl = mksem("s_cl")
    s_xg = [mksem(f"s_xg{i}") for i in range(3)]
    s_xf = [mksem(f"s_xf{i}") for i in range(2)]
    s_w = [mksem(f"s_w{i}") for i in range(2)]
    s_wc = mksem("s_wc")
    s_t = mksem("s_t")
    s_v = mksem("s_v")
    s_s = mksem("s_s")
    s_g2 = mksem("s_g2")
    s_sw = mksem("s_sw")
    s_saw = [mksem(f"s_saw{i}") for i in range(2)]
    s_sfw = [mksem(f"s_sfw{i}") for i in range(2)]
    s_sfr = [mksem(f"s_sfr{i}") for i in range(2)]
    s_esw = [mksem(f"s_esw{i}") for i in range(2)]
    s_wow = [mksem(f"s_wow{i}") for i in range(2)]
    s_wor = mksem("s_wor")
    s_esr = mksem("s_esr")
    s_so = [mksem(f"s_so{i}") for i in range(2)]
    s_d = mksem("s_d")
    s_ar2 = [mksem(f"s_ar2{i}") for i in range(2)]
    s_sfg = [mksem(f"s_sfg{i}") for i in range(2)]
    s_r2 = [mksem(f"s_r2{i}") for i in range(2)]

    C = {}

    def inc(instr, sem, n=1):
        instr.then_inc(sem, n)
        C[id(sem)] = C.get(id(sem), 0) + n
        return instr

    def cnt(sem):
        return C.get(id(sem), 0)

    class WS:
        def __init__(self):
            self.i_dma = 0
            self.i_use = 0
            self.held = 0

        def dma(self, g, src_ap):
            i = self.i_dma
            if i >= 2:
                g.wait_ge(s_wc, i - 1)
            dst = slot[i % 2].ap().rearrange("p (kb c) -> p kb c", c=512)
            inc(g.dma_start(out=dst, in_=src_ap), s_w[i % 2], 16)
            self.i_dma += 1

        def acquire(self, te):
            i = self.i_use + self.held
            assert self.held < 2
            te.wait_ge(s_w[i % 2], 16 * (i // 2 + 1))
            self.held += 1
            return slot[i % 2]

        def release(self, last_mm):
            inc(last_mm, s_wc, 1)
            self.i_use += 1
            self.held -= 1

    ws = WS()

    psum_es = ExitStack()
    gp = []

    def psum_open(n=8, tag=""):
        assert not gp
        for i in range(n):
            gp.append(psum_es.enter_context(nc.psum_tensor(f"gp_{tag}{i}", [128, 512], F32)))

    def psum_close():
        psum_es.pop_all().close()
        gp.clear()

    with nc.Block() as blk:
        @blk.sync
        def _(s):
            for dst, src in [
                (mask1, mask4_h[:, 0:128]), (idb, idb_h[:, :]), (idf, idf_h[:, :]),
                (evsc, evsc_h[:, :]), (odsc, odsc_h[:, :]),
                (evm, evmask_h[:, :]), (odm, odmask_h[:, :]),
                (onesb, onesb_h[:, :]), (onesf, onesf_h[:, :]),
                (convb, convb_h[:, :]),
                (bq_s, bq_h[:, :]),
                (bk_s, bk_h[:, :]),
                (bo_s, bo_h[:, :]),
                (projb, projb_h[:, :]),
                (gam, lng_h[:, :]),
                (bet, lnb_h[:, :]),
                (bv_row, bv_h.rearrange("h e -> (h e)")[None, :]),
            ]:
                inc(s.dma_start(out=dst[:], in_=src), s_cl, 16)
        N_CONST = cnt(s_cl)

        @blk.gpsimd
        def _(g):
            inc(g.dma_start(out=gate_sb[:], in_=gatew_h[:, :]), s_g2, 16)

        def emit_tap(name, srcs):
            if name not in taps:
                return
            waits = ([(s_v, cnt(s_v)), (s_t, cnt(s_t)), (s_s, cnt(s_s))] +
                     [(x, cnt(x)) for x in list(s_saw) + list(s_sfw) + list(s_esw)])

            @blk.sync
            def _(s):
                for sem, val in waits:
                    s.wait_ge(sem, val)
                for dst, src in srcs:
                    inc(s.dma_start(out=dst, in_=src), s_d, 16)

        st_es = ExitStack()
        ps_sum = [st_es.enter_context(nc.psum_tensor(f"pssum{i}", [1, 512], F32)) for i in range(3)]
        ps_sq = [st_es.enter_context(nc.psum_tensor(f"pssq{i}", [1, 512], F32)) for i in range(3)]
        chunks3 = [(0, 512), (512, 1024), (1024, 1026)]
        b_t = cnt(s_t)
        b_v = cnt(s_v)

        @blk.gpsimd
        def _(g):
            for k in range(KC_D):
                sl = k % 3
                if k >= 3:
                    g.wait_ge(s_t, b_t + k - 2)
                inc(g.dma_start(out=xb_sl[sl][:], in_=x_v[k]), s_xg[sl], 16)

        @blk.vector
        def _(v):
            for k in range(KC_D):
                sl = k % 3
                v.wait_ge(s_xg[sl], 16 * (k // 3 + 1))
                if k >= 3:
                    v.wait_ge(s_t, b_t + k - 2)
                inc(v.tensor_tensor(xq_sl[sl][:], xb_sl[sl][:], xb_sl[sl][:], ALU.mult), s_v, 1)

        @blk.tensor
        def _(te):
            te.wait_ge(s_cl, N_CONST)
            for k in range(KC_D):
                sl = k % 3
                te.wait_ge(s_xg[sl], 16 * (k // 3 + 1))
                te.wait_ge(s_v, b_v + k + 1)
                mm = None
                for ci, (c0, c1) in enumerate(chunks3):
                    te.matmul(ps_sum[ci][0:1, 0:c1 - c0], onesb[:], xb_sl[sl][:, c0:c1],
                              start=(k == 0), stop=(k == KC_D - 1))
                    mm = te.matmul(ps_sq[ci][0:1, 0:c1 - c0], onesb[:], xq_sl[sl][:, c0:c1],
                                   start=(k == 0), stop=(k == KC_D - 1))
                inc(mm, s_t, 1)

        def rowreg(which, c0, c1):
            t = {"MN": tsc[0], "VE": tsc[1], "RS": xf_sl[0], "T1": xf_sl[1]}[which]
            return t[0:1, c0:c1]

        def emit_ln_finalize(n, ps_sum_l, ps_sq_l, chunks_l, denom):
            @blk.vector
            def _(v):
                v.wait_ge(s_t, cnt(s_t))
                for ci, (c0, c1) in enumerate(chunks_l):
                    w = c1 - c0
                    v.tensor_scalar(rowreg("MN", c0, c1), ps_sum_l[ci][0:1, 0:w],
                                    -1.0 / denom, None, ALU.mult)
                    v.tensor_scalar(rowreg("VE", c0, c1), ps_sq_l[ci][0:1, 0:w],
                                    1.0 / denom, None, ALU.mult)
                v.tensor_tensor(rowreg("T1", 0, n), rowreg("MN", 0, n),
                                rowreg("MN", 0, n), ALU.mult)
                v.tensor_tensor(rowreg("VE", 0, n), rowreg("VE", 0, n),
                                rowreg("T1", 0, n), ALU.subtract)
                inc(v.tensor_scalar(rowreg("VE", 0, n), rowreg("VE", 0, n),
                                    EPS, None, ALU.add), s_v, 1)

            @blk.scalar
            def _(sc):
                sc.wait_ge(s_v, cnt(s_v))
                inc(sc.activation(rowreg("T1", 0, n), rowreg("VE", 0, n), AF.Sqrt),
                    s_s, 1)

            @blk.vector
            def _(v):
                v.wait_ge(s_s, cnt(s_s))
                inc(v.reciprocal(rowreg("RS", 0, n), rowreg("T1", 0, n)), s_sw, 1)
                v.wait_ge(s_sw, cnt(s_sw))
                v.tensor_tensor(rowreg("T1", 0, n), rowreg("RS", 0, n),
                                rowreg("RS", 0, n), ALU.mult)
                v.tensor_tensor(rowreg("T1", 0, n), rowreg("T1", 0, n),
                                rowreg("VE", 0, n), ALU.mult)
                v.tensor_scalar(rowreg("T1", 0, n), rowreg("T1", 0, n),
                                -0.5, 1.5, ALU.mult, ALU.add)
                inc(v.tensor_tensor(rowreg("RS", 0, n), rowreg("RS", 0, n),
                                    rowreg("T1", 0, n), ALU.mult), s_v, 1)

        emit_ln_finalize(NTH, ps_sum, ps_sq, chunks3, float(D))
        st_es.pop_all().close()

        bc_es = ExitStack()
        bc_ps = [bc_es.enter_context(nc.psum_tensor(f"bcps{i}", [128, 512], F32)) for i in range(2)]

        def emit_bcast(src_off, dst, n):
            b_t3 = cnt(s_t)
            b_v3 = cnt(s_v)
            nch = (n + 511) // 512

            @blk.tensor
            def _(te):
                te.wait_ge(s_v, b_v3)
                for ci in range(nch):
                    if ci >= 2:
                        te.wait_ge(s_v, b_v3 + ci - 1)
                    c0, c1 = ci * 512, min(n, (ci + 1) * 512)
                    inc(te.matmul(bc_ps[ci % 2][:, 0:c1 - c0], onesf[:],
                                  rowreg(src_off, c0, c1), start=True, stop=True), s_t, 1)

            @blk.vector
            def _(v):
                for ci in range(nch):
                    v.wait_ge(s_t, b_t3 + ci + 1)
                    c0, c1 = ci * 512, min(n, (ci + 1) * 512)
                    inc(v.tensor_copy(dst[:, c0:c1], bc_ps[ci % 2][:, 0:c1 - c0]), s_v, 1)

        emit_bcast("MN", mneg_bc, NTH)
        emit_bcast("RS", rstd_bc, NTH)
        bc_es.pop_all().close()

        b_v = cnt(s_v)
        b_xf = [cnt(s_xf[0]), cnt(s_xf[1])]

        @blk.sync
        def _(s):
            s.wait_ge(s_t, cnt(s_t))
            for k in range(KC_D):
                sl = k % 2
                if k >= 2:
                    s.wait_ge(s_v, b_v + k - 1)
                inc(s.dma_start(out=xf_sl[sl][:], in_=x_v[k]), s_xf[sl], 16)

        @blk.vector
        def _(v):
            for k in range(KC_D):
                sl = k % 2
                v.wait_ge(s_xf[sl], b_xf[sl] + 16 * (k // 2 + 1))
                v.tensor_tensor(xf_sl[sl][:], xf_sl[sl][:], mneg_bc[:], ALU.add)
                inc(v.tensor_tensor(xn(k), xf_sl[sl][:], rstd_bc[:], ALU.mult), s_v, 1)

        if "xn" in taps:
            emit_tap("xn", [(tap_h["xn"].rearrange("(mc p) t -> mc p t", p=128)[k], xn(k))
                            for k in range(KC_D)])

        psum_open(8, tag="a")
        conv_batches = [convw_v[w][:, kb * 8:(kb + 1) * 8, mg * 512:(mg + 1) * 512]
                        for mg in range(6) for w in range(3) for kb in range(4)]
        wc0 = cnt(s_wc)
        b_s = cnt(s_s)

        @blk.gpsimd
        def _(g):
            for src in conv_batches:
                ws.dma(g, src)

        @blk.tensor
        def _(te):
            te.wait_ge(s_v, cnt(s_v))
            for mg in range(6):
                if mg >= 1:
                    te.wait_ge(s_s, b_s + 8 * mg)
                last = None
                for w in range(3):
                    for kb in range(4):
                        sl = ws.acquire(te)
                        for kk in range(8):
                            k = kb * 8 + kk
                            for mj in range(4):
                                for nh in range(2):
                                    last = te.matmul(
                                        gp[mj * 2 + nh][:],
                                        sl[:, kk * 512 + mj * 128: kk * 512 + (mj + 1) * 128],
                                        xn(k)[:, w + nh * 512: w + nh * 512 + 512],
                                        start=(w == 0 and k == 0), stop=(w == 2 and k == 31))
                        ws.release(last)

        @blk.scalar
        def _(sc):
            sc.wait_ge(s_cl, N_CONST)
            for mg in range(6):
                sc.wait_ge(s_wc, wc0 + 12 * (mg + 1))
                for mj in range(4):
                    m = mg * 4 + mj
                    off = [B_Q, B_K, B_V][m // 8]
                    for nh in range(2):
                        dst = arB[:, off + (m % 8) * NT + nh * 512:
                                  off + (m % 8) * NT + nh * 512 + 512]
                        inc(sc.activation(dst, gp[mj * 2 + nh][:], AF.Identity,
                                          bias=convb[:, m:m + 1]), s_s, 1)

        if "qkv" in taps:
            emit_tap("qkv", [(tap_h["qkv"].rearrange("(mc p) t -> mc p t", p=128)[m],
                              arB[:, [B_Q, B_K, B_V][m // 8] + (m % 8) * NT:
                                  [B_Q, B_K, B_V][m // 8] + (m % 8 + 1) * NT])
                             for m in range(24)])

        @blk.vector
        def _(v):
            v.wait_ge(s_cl, N_CONST)
            v.tensor_scalar(bq_s[:], bq_s[:], 0.125, None, ALU.mult)
            v.tensor_scalar(bq_e[:], bq_s[:], evm[:, 0:1], None, ALU.mult)
            inc(v.tensor_scalar(bq_o[:], bq_s[:], odm[:, 0:1], None, ALU.mult), s_v, 1)

        def head_proj(w_view, bias_col, scale, src_off, dst_off, tapname,
                      masked=None):
            batches = [w_view[:, 0:8, mg * 512:(mg + 1) * 512] for mg in range(2)]
            wc0 = cnt(s_wc)
            b_s2 = cnt(s_s)
            b_v2 = cnt(s_v)

            @blk.gpsimd
            def _(g):
                for src in batches:
                    ws.dma(g, src)

            @blk.tensor
            def _(te):
                for mg in range(2):
                    te.wait_ge(s_s, b_s2 + 8 * mg)
                    sl = ws.acquire(te)
                    last = None
                    for kk in range(8):
                        for mj in range(4):
                            for nh in range(2):
                                last = te.matmul(
                                    gp[mj * 2 + nh][:],
                                    sl[:, kk * 512 + mj * 128: kk * 512 + (mj + 1) * 128],
                                    arB[:, src_off + kk * NT + nh * 512:
                                        src_off + kk * NT + nh * 512 + 512],
                                    start=(kk == 0), stop=(kk == 7))
                    ws.release(last)

            @blk.scalar
            def _(sc):
                sc.wait_ge(s_v, b_v2)
                for mg in range(2):
                    sc.wait_ge(s_wc, wc0 + mg + 1)
                    for mj in range(4):
                        m = mg * 4 + mj
                        for nh in range(2):
                            if masked is None:
                                dst = arB[:, dst_off + m * NT + nh * 512:
                                          dst_off + m * NT + nh * 512 + 512]
                                inc(sc.activation(dst, gp[mj * 2 + nh][:], AF.Identity,
                                                  bias=bias_col[:, m:m + 1], scale=scale),
                                    s_s, 1)
                            else:
                                de, do = masked
                                dste = arB[:, de + m * NT + nh * 512:
                                           de + m * NT + nh * 512 + 512]
                                dsto = arB[:, do + m * NT + nh * 512:
                                           do + m * NT + nh * 512 + 512]
                                sc.activation(dste, gp[mj * 2 + nh][:], AF.Identity,
                                              bias=bq_e[:, m:m + 1], scale=evsc[:, 0:1])
                                inc(sc.activation(dsto, gp[mj * 2 + nh][:], AF.Identity,
                                                  bias=bq_o[:, m:m + 1], scale=odsc[:, 0:1]),
                                    s_s, 1)

            if tapname in taps:
                emit_tap(tapname, [(tap_h[tapname].rearrange("(mc p) t -> mc p t", p=128)[m],
                                    arB[:, dst_off + m * NT: dst_off + (m + 1) * NT])
                                   for m in range(8)])

        head_proj(wq_v, bq_s, 0.125, B_Q, B_QH, "qh", masked=(B_QH, B_QHO))
        head_proj(wk_v, bk_s, 1.0, B_K, B_KH, "kh")

        @blk.tensor
        def _(te):
            te.wait_ge(s_s, cnt(s_s))
            te.matmul(gp[0][:], onesf[:], bv_row[0:1, 0:512], start=True, stop=True)
            inc(te.matmul(gp[1][:], onesf[:], bv_row[0:1, 512:1024], start=True, stop=True),
                s_t, 1)

        @blk.vector
        def _(v):
            v.wait_ge(s_t, cnt(s_t))
            v.tensor_copy(bv_bc[:, 0:512], gp[0][:])
            inc(v.tensor_copy(bv_bc[:, 512:1024], gp[1][:]), s_v, 1)

        b_t = cnt(s_t)
        b_v = cnt(s_v)

        @blk.gpsimd
        def _(g):
            for nh in range(2):
                ws.dma(g, wv_v[:, 0:8, nh * 512:(nh + 1) * 512])

        @blk.tensor
        def _(te):
            te.wait_ge(s_v, b_v)
            sl0 = ws.acquire(te)
            sl1 = ws.acquire(te)
            last = None
            for b in range(8):
                pr = (b % 2) * 2
                if b >= 2:
                    te.wait_ge(s_v, b_v + (b - 1))
                for nh, sl in ((0, sl0), (1, sl1)):
                    for k in range(8):
                        last = te.matmul(gp[4 + pr + nh][:],
                                         arB[:, B_V + k * NT + b * 128: B_V + k * NT + (b + 1) * 128],
                                         sl[:, k * 512:(k + 1) * 512],
                                         start=(k == 0), stop=(k == 7))
                inc(last, s_t, 1)
            ws.i_use += 2
            ws.held -= 2

        @blk.vector
        def _(v):
            for b in range(8):
                v.wait_ge(s_t, b_t + b + 1)
                pr = (b % 2) * 2
                for nh in range(2):
                    dst = arB[:, B_VH + b * 1024 + nh * 512: B_VH + b * 1024 + (nh + 1) * 512]
                    cp = v.tensor_tensor(dst, gp[4 + pr + nh][:],
                                         bv_bc[:, nh * 512:(nh + 1) * 512], ALU.add)
                    if b == 7:
                        inc(cp, s_wc, 1)
                    elif nh == 1:
                        inc(cp, s_v, 1)
                if b == 7:
                    inc(v.tensor_copy(sm8[:, 15:16], bv_bc[:, 0:1]), s_v, 1)

        if "vh" in taps:
            emit_tap("vh", [(tap_h["vh"].rearrange("(b p) w -> b p w", p=128)[b],
                             arB[:, B_VH + b * 1024: B_VH + (b + 1) * 1024])
                            for b in range(8)])

        psum_close()
        at_es = ExitStack()
        ps_sc = [at_es.enter_context(nc.psum_tensor(f"psc{i}", [128, 512], F32)) for i in range(2)]
        ps_pt = at_es.enter_context(nc.psum_tensor("pspt", [128, 1024], BF16))
        ps_o = at_es.enter_context(nc.psum_tensor("pso", [128, 512], F32))
        ps_tr = [at_es.enter_context(nc.psum_tensor(f"pstr{i}", [128, 128], BF16))
                 for i in range(2)]
        sc_f, e_f = spw[0], spw[1]
        b_t = cnt(s_t)
        b_v = cnt(s_v)
        b_s = cnt(s_s)

        @blk.tensor
        def _(te):
            for wv_i in range(16):
                b, w = wv_i // 2, wv_i % 2
                te.wait_ge(s_v, b_v + 4 * wv_i)
                mm = None
                for j in range(8):
                    h = w * 8 + j
                    qb = B_QH if h % 2 == 0 else B_QHO
                    qs = arB[:, qb + (h // 2) * NT + b * 128:
                             qb + (h // 2) * NT + (b + 1) * 128]
                    ks = arB[:, B_KH + (h // 2) * NT + b * 128:
                             B_KH + (h // 2) * NT + (b + 1) * 128]
                    mm = te.matmul(ps_sc[j // 4][:, (j % 4) * 128:(j % 4 + 1) * 128],
                                   qs, ks, start=(j % 4 == 0), stop=(j % 4 == 3))
                inc(mm, s_t, 1)
                te.wait_ge(s_v, b_v + 4 * wv_i + 2)
                for j in range(8):
                    te.matmul(ps_pt[:, j * 128:(j + 1) * 128],
                              arB[:, B_P + j * 128: B_P + (j + 1) * 128], idb[:],
                              is_transpose=True, start=(j == 0), stop=(j == 7))
                inc(te.matmul(ps_sc[0][0:1, 0:1], onesb[0:1, 0:1], onesb[0:1, 0:1],
                              start=True, stop=True), s_t, 1)
                te.wait_ge(s_v, b_v + 4 * wv_i + 3)
                mm = None
                for j in range(8):
                    h = w * 8 + j
                    vs = arB[:, B_VH + b * 1024 + h * 64: B_VH + b * 1024 + (h + 1) * 64]
                    mm = te.matmul(ps_o[:, j * 64:(j + 1) * 64],
                                   arB[:, B_PT + j * 128: B_PT + (j + 1) * 128], vs,
                                   start=(j == 0), stop=(j == 7))
                inc(mm, s_t, 1)

        @blk.vector
        def _(v):
            for wv_i in range(16):
                b, w = wv_i // 2, wv_i % 2
                v.wait_ge(s_t, b_t + 3 * wv_i + 1)
                v.wait_ge(s_s, b_s + wv_i)
                for j in range(8):
                    v.tensor_tensor(sc_f[:, j * 128:(j + 1) * 128],
                                    ps_sc[j // 4][:, (j % 4) * 128:(j % 4 + 1) * 128],
                                    mask1[:], ALU.add)
                inc(v.reduce_max(sm8[:, 8:9], sc_f[:], axis=AX.X, negate=True), s_v, 1)
                v.wait_ge(s_s, b_s + wv_i + 1)
                inc(v.reduce_sum(sm8[:, 0:8],
                                 e_f[:].rearrange("p (j c) -> p j c", c=128),
                                 axis=AX.X), s_sw, 1)
                v.wait_ge(s_sw, cnt(s_sw))
                inc(v.reciprocal(sm8[:, 8:16], sm8[:, 0:8]), s_sw, 1)
                v.wait_ge(s_sw, cnt(s_sw))
                cp = None
                for j in range(8):
                    cp = v.tensor_scalar(arB[:, B_P + j * 128: B_P + (j + 1) * 128],
                                         e_f[:, j * 128:(j + 1) * 128], sm8[:, 8 + j:9 + j],
                                         None, ALU.mult)
                inc(cp, s_v, 1)
                v.wait_ge(s_t, b_t + 3 * wv_i + 2)
                inc(v.tensor_copy(arB[:, B_PT:B_PT + 1024], ps_pt[:]), s_v, 1)
                v.wait_ge(s_t, b_t + 3 * wv_i + 3)
                inc(v.tensor_copy(arB[:, B_O + b * 1024 + w * 512:
                                      B_O + b * 1024 + (w + 1) * 512], ps_o[:]), s_v, 1)

        @blk.scalar
        def _(sc):
            for wv_i in range(16):
                sc.wait_ge(s_v, b_v + 4 * wv_i + 1)
                inc(sc.activation(e_f[:], sc_f[:], AF.Exp, bias=sm8[:, 8:9]), s_s, 1)

        pairs = [(b, t) for b in range(8) for t in range(b, 8)]
        b_t = cnt(s_t)
        b_v = cnt(s_v)

        @blk.tensor
        def _(te):
            te.wait_ge(s_v, b_v)
            for i, (b, t) in enumerate(pairs):
                if i >= 1:
                    te.wait_ge(s_v, b_v + i)
                te.transpose(ps_tr[0][:], arB[:, B_O + b * 1024 + t * 128:
                                               B_O + b * 1024 + (t + 1) * 128], idb[:])
                if b != t:
                    te.transpose(ps_tr[1][:], arB[:, B_O + t * 1024 + b * 128:
                                                   B_O + t * 1024 + (b + 1) * 128], idb[:])
                inc(te.matmul(ps_sc[0][0:1, 0:1], onesb[0:1, 0:1], onesb[0:1, 0:1],
                              start=True, stop=True), s_t, 1)

        @blk.vector
        def _(v):
            for i, (b, t) in enumerate(pairs):
                v.wait_ge(s_t, b_t + i + 1)
                cp = v.tensor_copy(arB[:, B_O + t * 1024 + b * 128:
                                       B_O + t * 1024 + (b + 1) * 128], ps_tr[0][:])
                if b != t:
                    cp = v.tensor_copy(arB[:, B_O + b * 1024 + t * 128:
                                           B_O + b * 1024 + (t + 1) * 128], ps_tr[1][:])
                inc(cp, s_v, 1)

        if "o" in taps:
            emit_tap("o", [(tap_h["o"].rearrange("(mc p) t -> mc p t", p=128)[m],
                            arB[:, B_O + m * NT: B_O + (m + 1) * NT]) for m in range(8)])

        at_es.pop_all().close()
        psum_open(8, tag="b")
        wc0 = cnt(s_wc)
        b_s = cnt(s_s)

        @blk.gpsimd
        def _(g):
            for mg in range(2):
                ws.dma(g, wo_v[:, 0:8, mg * 512:(mg + 1) * 512])

        @blk.tensor
        def _(te):
            te.wait_ge(s_v, cnt(s_v))
            for mg in range(2):
                if mg >= 1:
                    te.wait_ge(s_s, b_s + 8 * mg)
                sl = ws.acquire(te)
                last = None
                for kk in range(8):
                    for mj in range(4):
                        for nh in range(2):
                            last = te.matmul(gp[mj * 2 + nh][:],
                                             sl[:, kk * 512 + mj * 128: kk * 512 + (mj + 1) * 128],
                                             arB[:, B_O + kk * NT + nh * 512:
                                                 B_O + kk * NT + nh * 512 + 512],
                                             start=(kk == 0), stop=(kk == 7))
                ws.release(last)

        b_wow = [cnt(s_wow[0]), cnt(s_wow[1])]
        b_s_wo = cnt(s_s)

        @blk.scalar
        def _(sc):
            for mg in range(2):
                sc.wait_ge(s_wc, wc0 + mg + 1)
                for mj in range(4):
                    m = mg * 4 + mj
                    if m >= 2:
                        sc.wait_ge(s_wow[m % 2], b_wow[m % 2] + 16 * (m // 2))
                    for nh in range(2):
                        dst = arB[:, WO_STG + (m % 2) * NT + nh * 512:
                                  WO_STG + (m % 2) * NT + nh * 512 + 512]
                        inc(sc.activation(dst, gp[mj * 2 + nh][:], AF.Identity,
                                          bias=bo_s[:, m:m + 1]), s_s, 1)

        @blk.sync
        def _(s):
            for m in range(8):
                s.wait_ge(s_s, b_s_wo + 2 * (m + 1))
                inc(s.dma_start(out=wosp_v[m],
                                in_=arB[:, WO_STG + (m % 2) * NT: WO_STG + (m % 2 + 1) * NT]),
                    s_wow[m % 2], 16)

        if "wo_out" in taps:
            emit_tap("wo_out", [(tap_h["wo_out"][:, :], wosp_h[:, :])])

        b_t = cnt(s_t)
        b_v = cnt(s_v)

        @blk.tensor
        def _(te):
            te.wait_ge(s_g2, 16)
            te.wait_ge(s_s, cnt(s_s))
            for b in range(8):
                if b >= 1:
                    te.wait_ge(s_v, b_v + b)
                mm = None
                for k in range(KC_D):
                    mm = te.matmul(gp[0][:, 0:8],
                                   xn(k)[:, 1 + b * 128: 1 + (b + 1) * 128],
                                   gate_sb[:, k * 8:(k + 1) * 8],
                                   start=(k == 0), stop=(k == KC_D - 1))
                inc(mm, s_t, 1)

        @blk.vector
        def _(v):
            def F(op):
                inc(op, s_sw, 1)
                v.wait_ge(s_sw, cnt(s_sw))
            for b in range(8):
                v.wait_ge(s_t, b_t + b + 1)
                F(v.tensor_copy(gsb[:], gp[0][:, 0:8]))
                F(v.reduce_max(tg2[:], gsb[:], axis=AX.X))
                F(v.tensor_scalar(tg[:], gsb[:], tg2[:, 0:1], None, ALU.is_equal))
                F(v.reduce_sum(tg3[:], tg[:], axis=AX.X))
                F(v.scalar_tensor_tensor(tg[:], tg[:], 1e30, gsb[:], ALU.mult, ALU.subtract))
                F(v.tensor_scalar(tg[:], tg[:], -1.0, None, ALU.mult))
                F(v.reduce_max(tg4[:], tg[:], axis=AX.X))
                F(v.tensor_scalar(tg3[:], tg3[:], 2.0, None, ALU.is_ge))
                F(v.tensor_tensor(route_cols[:, b:b + 1], tg2[:], tg4[:], ALU.subtract))
                F(v.tensor_tensor(route_cols[:, b:b + 1], route_cols[:, b:b + 1], tg3[:],
                                  ALU.mult))
                F(v.tensor_tensor(route_cols[:, b:b + 1], route_cols[:, b:b + 1], tg4[:],
                                  ALU.add))
                inc(v.tensor_tensor(route_cols[:, b:b + 1], route_cols[:, b:b + 1], tg2[:],
                                    ALU.add), s_v, 1)

        @blk.tensor
        def _(te):
            te.wait_ge(s_v, cnt(s_v))
            te.transpose(gp[1][0:8, 0:128], route_cols[:], idf[:])
            inc(te.matmul(gp[2][0:1, 0:1], onesb[0:1, 0:1], onesb[0:1, 0:1],
                          start=True, stop=True), s_t, 1)

        @blk.vector
        def _(v):
            v.wait_ge(s_t, cnt(s_t))
            inc(v.tensor_copy(row8[0:8, 0:128], gp[1][0:8, 0:128]), s_v, 1)

        b_g2 = cnt(s_g2)

        @blk.gpsimd
        def _(g):
            g.wait_ge(s_v, cnt(s_v))
            for j in range(8):
                inc(g.dma_start(out=route_row[0:1, j * 128:(j + 1) * 128],
                                in_=row8[j:j + 1, 0:128]), s_g2, 16)

        @blk.tensor
        def _(te):
            te.wait_ge(s_g2, cnt(s_g2))
            te.matmul(gp[2][:], onesf[:], route_row[0:1, 0:512], start=True, stop=True)
            inc(te.matmul(gp[3][:], onesf[:], route_row[0:1, 512:1024], start=True, stop=True),
                s_t, 1)

        @blk.vector
        def _(v):
            v.wait_ge(s_t, cnt(s_t))
            v.tensor_copy(route_bc[:, 0:512], gp[2][:])
            inc(v.tensor_copy(route_bc[:, 512:1024], gp[3][:]), s_v, 1)

        if "route" in taps:
            emit_tap("route", [(tap_h["route"][:, :], route_bc[:])])

        wc0 = cnt(s_wc)
        b_v = cnt(s_v)

        @blk.gpsimd
        def _(g):
            for mg in range(4):
                for kb in range(4):
                    ws.dma(g, fc1w_v[:, kb * 8:(kb + 1) * 8, mg * 512:(mg + 1) * 512])

        @blk.tensor
        def _(te):
            te.wait_ge(s_v, b_v)
            for mg in range(4):
                if mg >= 1:
                    te.wait_ge(s_v, b_v + 8 * mg)
                last = None
                for kb in range(4):
                    sl = ws.acquire(te)
                    for kk in range(8):
                        k = kb * 8 + kk
                        for mj in range(4):
                            for nh in range(2):
                                last = te.matmul(
                                    gp[mj * 2 + nh][:],
                                    sl[:, kk * 512 + mj * 128: kk * 512 + (mj + 1) * 128],
                                    xn(k)[:, 1 + nh * 512: 1 + nh * 512 + 512],
                                    start=(k == 0), stop=(k == KC_D - 1))
                    ws.release(last)

        @blk.vector
        def _(v):
            for mg in range(4):
                v.wait_ge(s_wc, wc0 + 4 * (mg + 1))
                for mj in range(4):
                    m = mg * 4 + mj
                    for nh in range(2):
                        dst = arB[:, B_XLAT + m * NT + nh * 512: B_XLAT + m * NT + nh * 512 + 512]
                        inc(v.tensor_copy(dst, gp[mj * 2 + nh][:]), s_v, 1)

        if "xlat" in taps:
            emit_tap("xlat", [(tap_h["xlat"].rearrange("(mc p) t -> mc p t", p=128)[m],
                               arB[:, B_XLAT + m * NT: B_XLAT + (m + 1) * NT])
                              for m in range(16)])

        wc0 = cnt(s_wc)
        b_v = cnt(s_v)
        b_esw = [cnt(s_esw[0]), cnt(s_esw[1])]
        esc_stage = arB[:, ESC_STG:ESC_STG + 2 * NT]

        @blk.gpsimd
        def _(g):
            for mg in range(4):
                for kb in range(2):
                    ws.dma(g, expw_v[:, kb * 8:(kb + 1) * 8, mg * 512:(mg + 1) * 512])

        @blk.tensor
        def _(te):
            te.wait_ge(s_v, b_v)
            for mg in range(4):
                if mg >= 1:
                    te.wait_ge(s_v, b_v + 8 * mg)
                last = None
                for kb in range(2):
                    sl = ws.acquire(te)
                    for kk in range(8):
                        k = kb * 8 + kk
                        for mj in range(4):
                            for nh in range(2):
                                last = te.matmul(
                                    gp[mj * 2 + nh][:],
                                    sl[:, kk * 512 + mj * 128: kk * 512 + (mj + 1) * 128],
                                    arB[:, B_XLAT + k * NT + nh * 512:
                                        B_XLAT + k * NT + nh * 512 + 512],
                                    start=(k == 0), stop=(k == 15))
                    ws.release(last)

        @blk.vector
        def _(v):
            for mg in range(4):
                v.wait_ge(s_wc, wc0 + 2 * (mg + 1))
                for mj in range(4):
                    m = mg * 4 + mj
                    if m >= 2:
                        v.wait_ge(s_esw[m % 2], b_esw[m % 2] + 16 * (m // 2))
                    cp = None
                    for nh in range(2):
                        dst = esc_stage[:, (m % 2) * NT + nh * 512: (m % 2) * NT + nh * 512 + 512]
                        cp = v.tensor_tensor(dst, gp[mj * 2 + nh][:],
                                             route_bc[:, nh * 512:(nh + 1) * 512], ALU.mult)
                    inc(cp, s_v, 2)

        b_v_e = b_v

        @blk.sync
        def _(s):
            for m in range(16):
                s.wait_ge(s_v, b_v_e + 2 * (m + 1))
                inc(s.dma_start(out=esc_v[:, m, :],
                                in_=esc_stage[:, (m % 2) * NT:(m % 2 + 1) * NT]), s_esw[m % 2], 16)

        if "esc" in taps:
            emit_tap("esc", [(tap_h["esc"][:, :], esc_h[:, :])])

        def up_half(half):
            wc0 = cnt(s_wc)
            b_s2 = cnt(s_s)
            b_v2 = cnt(s_v)

            @blk.gpsimd
            def _(g):
                for mg in range(8):
                    for kb in range(4):
                        ws.dma(g, upw_v[:, kb * 8:(kb + 1) * 8,
                                        half * 4096 + mg * 512: half * 4096 + (mg + 1) * 512])

            @blk.tensor
            def _(te):
                te.wait_ge(s_v, b_v2)
                for mg in range(8):
                    if mg >= 1:
                        te.wait_ge(s_s, b_s2 + 8 * mg)
                    last = None
                    for kb in range(4):
                        sl = ws.acquire(te)
                        for kk in range(8):
                            k = kb * 8 + kk
                            for mj in range(4):
                                for nh in range(2):
                                    last = te.matmul(
                                        gp[mj * 2 + nh][:],
                                        sl[:, kk * 512 + mj * 128: kk * 512 + (mj + 1) * 128],
                                        xn(k)[:, 1 + nh * 512: 1 + nh * 512 + 512],
                                        start=(k == 0), stop=(k == KC_D - 1))
                        ws.release(last)

            @blk.scalar
            def _(sc):
                for mg in range(8):
                    sc.wait_ge(s_wc, wc0 + 4 * (mg + 1))
                    for mj in range(4):
                        m = mg * 4 + mj
                        if m >= 2:
                            sc.wait_ge(s_v, b_v2 + 2 * (m - 1))
                        for nh in range(2):
                            inc(sc.activation(tsc[m % 2][:, nh * 512:(nh + 1) * 512],
                                              gp[mj * 2 + nh][:], AF.Relu), s_s, 1)

            @blk.vector
            def _(v):
                for m in range(32):
                    v.wait_ge(s_s, b_s2 + 2 * (m + 1))
                    cp = None
                    for nh in range(2):
                        src = tsc[m % 2][:, nh * 512:(nh + 1) * 512]
                        dst = arB[:, B_H + m * NT + nh * 512: B_H + m * NT + nh * 512 + 512]
                        cp = v.tensor_tensor(dst, src, src, ALU.mult)
                    inc(cp, s_v, 2)

        def down_pass(half, add_fc2):
            wc0 = cnt(s_wc)
            bpg = 6 if add_fc2 else 4
            b_v2 = cnt(s_v)
            wsem = s_sfw if add_fc2 else s_saw
            b_w2 = [cnt(wsem[0]), cnt(wsem[1])]
            tgt_v = sf_v if add_fc2 else sa_v
            b_xfl = [cnt(s_xf[0]), cnt(s_xf[1])]
            n_saw_total = [cnt(s_saw[0]), cnt(s_saw[1])]

            @blk.gpsimd
            def _(g):
                for mg in range(8):
                    for kb in range(4):
                        ws.dma(g, downw_v[:, half * 32 + kb * 8: half * 32 + (kb + 1) * 8,
                                          mg * 512:(mg + 1) * 512])
                    if add_fc2:
                        for kb in range(2):
                            ws.dma(g, fc2w_v[:, kb * 8:(kb + 1) * 8, mg * 512:(mg + 1) * 512])

            if add_fc2:
                @blk.scalar
                def _(sc):
                    sc.wait_ge(s_saw[0], n_saw_total[0])
                    sc.wait_ge(s_saw[1], n_saw_total[1])
                    for m in range(32):
                        sl = m % 2
                        if m >= 2:
                            sc.wait_ge(s_v, b_v2 + m - 1)
                        inc(sc.dma_start(out=xf_sl[sl][:, 0:NT], in_=sa_v[m]),
                            s_ar2[sl], 16)

                @blk.sync
                def _(s):
                    for m in range(32):
                        s.wait_ge(s_v, b_v2 + m + 1)
                        inc(s.dma_start(out=tgt_v[m], in_=spw[m % 2][:]), wsem[m % 2], 16)

            b_s3 = cnt(s_s)

            @blk.tensor
            def _(te):
                te.wait_ge(s_s, b_s3)
                for mg in range(8):
                    if mg >= 1:
                        te.wait_ge(s_v, b_v2 + 4 * mg)
                    last = None
                    for kb in range(4):
                        sl = ws.acquire(te)
                        for kk in range(8):
                            k = kb * 8 + kk
                            for mj in range(4):
                                for nh in range(2):
                                    last = te.matmul(
                                        gp[mj * 2 + nh][:],
                                        sl[:, kk * 512 + mj * 128: kk * 512 + (mj + 1) * 128],
                                        arB[:, B_H + k * NT + nh * 512:
                                            B_H + k * NT + nh * 512 + 512],
                                        start=(k == 0), stop=(not add_fc2 and k == KC_D - 1))
                        ws.release(last)
                    if add_fc2:
                        if mg == 0:
                            te.wait_ge(s_esr, 32)
                        for kb in range(2):
                            sl = ws.acquire(te)
                            for kk in range(8):
                                k = kb * 8 + kk
                                for mj in range(4):
                                    for nh in range(2):
                                        last = te.matmul(
                                            gp[mj * 2 + nh][:],
                                            sl[:, kk * 512 + mj * 128: kk * 512 + (mj + 1) * 128],
                                            arA[:, A_ESC + k * NT + nh * 512:
                                                A_ESC + k * NT + nh * 512 + 512],
                                            start=False, stop=(k == 15))
                            ws.release(last)

            @blk.vector
            def _(v):
                for mg in range(8):
                    v.wait_ge(s_wc, wc0 + bpg * (mg + 1))
                    for mj in range(4):
                        m = mg * 4 + mj
                        if m >= 2:
                            v.wait_ge(wsem[m % 2], b_w2[m % 2] + 16 * (m // 2))
                        if add_fc2:
                            v.wait_ge(s_ar2[m % 2], 16 * (m // 2 + 1))
                        ins = None
                        for nh in range(2):
                            dst = spw[m % 2][:, nh * 512:(nh + 1) * 512]
                            if add_fc2:
                                ins = v.tensor_tensor(dst, gp[mj * 2 + nh][:],
                                                      xf_sl[m % 2][:, nh * 512:(nh + 1) * 512],
                                                      ALU.add)
                            else:
                                ins = v.tensor_copy(dst, gp[mj * 2 + nh][:])
                        inc(ins, s_v, 1)

            if not add_fc2:
                @blk.sync
                def _(s):
                    for m in range(32):
                        s.wait_ge(s_v, b_v2 + m + 1)
                        inc(s.dma_start(out=tgt_v[m], in_=spw[m % 2][:]), wsem[m % 2], 16)

        up_half(0)
        down_pass(0, add_fc2=False)
        if "sa" in taps:
            emit_tap("sa", [(tap_h["sa"][:, :], sa_h[:, :])])
        if "h" in taps:
            emit_tap("h", [(tap_h["h"].rearrange("(mc p) t -> mc p t", p=128)[m],
                            arB[:, B_H + m * NT: B_H + (m + 1) * NT]) for m in range(32)])
        up_half(1)

        b_esr_wc = cnt(s_wc)

        @blk.sync
        def _(s):
            s.wait_ge(s_wc, b_esr_wc)
            s.wait_ge(s_esw[0], cnt(s_esw[0]))
            s.wait_ge(s_esw[1], cnt(s_esw[1]))
            for half in range(2):
                inc(s.dma_start(
                    out=arA[:, 0:16 * NT].rearrange("p (mc t) -> p mc t", t=NT)[
                        :, half * 8:(half + 1) * 8, :],
                    in_=esc_v[:, half * 8:(half + 1) * 8, :]), s_esr, 16)

        down_pass(1, add_fc2=True)
        if "sf" in taps:
            emit_tap("sf", [(tap_h["sf"][:, :], sf_h[:, :])])

        psum_close()
        st2_es = ExitStack()
        ps_sum2 = [st2_es.enter_context(nc.psum_tensor(f"ps2sum{i}", [1, 512], F32)) for i in range(2)]
        ps_sq2 = [st2_es.enter_context(nc.psum_tensor(f"ps2sq{i}", [1, 512], F32)) for i in range(2)]
        chunks2 = [(0, 512), (512, 1024)]
        b_t = cnt(s_t)
        b_v = cnt(s_v)
        b_sfw_tot = [cnt(s_sfw[0]), cnt(s_sfw[1])]
        b_sfr = [cnt(s_sfr[0]), cnt(s_sfr[1])]

        @blk.gpsimd
        def _(g):
            for m in range(KC_D):
                sl = m % 2
                if m >= 2:
                    g.wait_ge(s_t, b_t + m - 1)
                g.wait_ge(s_sfw[sl], b_sfw_tot[sl] - 256 + 16 * (m // 2 + 1))
                inc(g.dma_start(out=arB[:, SF_STG + sl * NT: SF_STG + (sl + 1) * NT],
                                in_=sf_v[m]), s_sfg[sl], 16)

        @blk.vector
        def _(v):
            for m in range(KC_D):
                sl = m % 2
                v.wait_ge(s_sfg[sl], 16 * (m // 2 + 1))
                if m >= 2:
                    v.wait_ge(s_t, b_t + m - 1)
                inc(v.tensor_tensor(arB[:, SF_STG + (2 + sl) * NT: SF_STG + (3 + sl) * NT],
                                    arB[:, SF_STG + sl * NT: SF_STG + (sl + 1) * NT],
                                    arB[:, SF_STG + sl * NT: SF_STG + (sl + 1) * NT],
                                    ALU.mult), s_v, 1)

        @blk.tensor
        def _(te):
            for m in range(KC_D):
                te.wait_ge(s_v, b_v + m + 1)
                mm = None
                for ci, (c0, c1) in enumerate(chunks2):
                    te.matmul(ps_sum2[ci][0:1, :], onesb[:],
                              arB[:, SF_STG + (m % 2) * NT + c0: SF_STG + (m % 2) * NT + c1],
                              start=(m == 0), stop=(m == KC_D - 1))
                    mm = te.matmul(ps_sq2[ci][0:1, :], onesb[:],
                                   arB[:, SF_STG + (2 + m % 2) * NT + c0:
                                       SF_STG + (2 + m % 2) * NT + c1],
                                   start=(m == 0), stop=(m == KC_D - 1))
                inc(mm, s_t, 1)

        emit_ln_finalize(NT, ps_sum2, ps_sq2, chunks2, float(D))
        st2_es.pop_all().close()
        bc_es2 = ExitStack()
        bc_ps = [bc_es2.enter_context(nc.psum_tensor(f"bc2ps{i}", [128, 512], F32)) for i in range(2)]
        emit_bcast("MN", mneg_bc, NT)
        emit_bcast("RS", rstd_bc, NT)
        bc_es2.pop_all().close()

        psum_open(8, tag="c")
        wc0 = cnt(s_wc)
        b_v = cnt(s_v)
        b_s = cnt(s_s)
        b_so = [cnt(s_so[0]), cnt(s_so[1])]
        b_xfl = [cnt(s_xf[0]), cnt(s_xf[1])]
        b_sfr = [cnt(s_sfr[0]), cnt(s_sfr[1])]

        spw3 = [spw[0], spw[1], bcast_tok]
        sfsem = [s_r2[0], s_r2[1], s_sfr[0]]
        b_sf3 = [cnt(sm) for sm in sfsem]
        b_xg = [cnt(s_xg[0]), cnt(s_xg[1])]

        @blk.gpsimd
        def _(g):
            for mg in range(8):
                ws.dma(g, projw_v[:, 0:8, mg * 512:(mg + 1) * 512])
                for mj in range(4):
                    m = mg * 4 + mj
                    if m >= 2:
                        g.wait_ge(s_v, b_v + 2 * m - 2)
                    else:
                        g.wait_ge(s_v, b_v)
                    inc(g.dma_start(out=xf_sl[m % 2][:, 0:NT], in_=x_v[m][:, 1:1 + NT]),
                        s_xg[m % 2], 16)

        @blk.sync
        def _(s):
            s.wait_ge(s_t, cnt(s_t))
            s.wait_ge(s_wow[0], cnt(s_wow[0]))
            s.wait_ge(s_wow[1], cnt(s_wow[1]))
            inc(s.dma_start(out=arB[:, WO_R:WO_R + 8 * NT].rearrange(
                    "p (mc t) -> p mc t", t=NT), in_=wosp_r), s_wor, 16)
            for m in range(KC_D):
                s.wait_ge(s_v, b_v + 2 * m + 2)
                inc(s.dma_start(out=out_v[m], in_=spw3[m % 3][:]), s_so[m % 2], 16)
            s.wait_ge(s_so[0], b_so[0] + 16 * (KC_D // 2))
            s.wait_ge(s_so[1], b_so[1] + 16 * (KC_D // 2))

        @blk.tensor
        def _(te):
            te.wait_ge(s_v, b_v)
            te.wait_ge(s_wor, cnt(s_wor))
            for mg in range(8):
                if mg >= 1:
                    te.wait_ge(s_v, b_v + 8 * mg)
                sl = ws.acquire(te)
                last = None
                for kk in range(8):
                    for mj in range(4):
                        for nh in range(2):
                            last = te.matmul(gp[mj * 2 + nh][:],
                                             sl[:, kk * 512 + mj * 128: kk * 512 + (mj + 1) * 128],
                                             arB[:, WO_R + kk * NT + nh * 512:
                                                 WO_R + kk * NT + nh * 512 + 512],
                                             start=(kk == 0), stop=(kk == 7))
                ws.release(last)

        @blk.vector
        def _(v):
            for mg in range(8):
                v.wait_ge(s_wc, wc0 + mg + 1)
                for mj in range(4):
                    m = mg * 4 + mj
                    sl = m % 2
                    s3 = m % 3
                    v.wait_ge(sfsem[s3], b_sf3[s3] + 16 * (m // 3 + 1))
                    v.wait_ge(s_xg[sl], b_xg[sl] + 16 * (m // 2 + 1))
                    ins = None
                    for nh in range(2):
                        cs = slice(nh * 512, (nh + 1) * 512)
                        v.tensor_tensor(tsc[0][:, cs], spw3[s3][:, cs], mneg_bc[:, cs], ALU.add)
                        ins = v.tensor_tensor(tsc[0][:, cs], tsc[0][:, cs], rstd_bc[:, cs],
                                              ALU.mult)
                    inc(ins, s_v, 1)
                    v.wait_ge(s_s, b_s + m + 1)
                    ins = None
                    for nh in range(2):
                        cs = slice(nh * 512, (nh + 1) * 512)
                        v.tensor_tensor(tsc[1][:, cs], tsc[1][:, cs], xf_sl[sl][:, cs], ALU.add)
                        ins = v.scalar_tensor_tensor(spw3[s3][:, cs], gp[mj * 2 + nh][:],
                                                     projb[:, m:m + 1], tsc[1][:, cs],
                                                     ALU.add, ALU.add)
                    inc(ins, s_v, 1)

        @blk.scalar
        def _(sc):
            for m in range(KC_D + 1):
                if m < KC_D:
                    s3 = m % 3
                    if m >= 3:
                        sp = (m - 3) % 2
                        sc.wait_ge(s_so[sp], b_so[sp] + 16 * ((m - 3) // 2 + 1))
                    else:
                        sc.wait_ge(s_v, b_v)
                    inc(sc.dma_start(out=spw3[s3][:], in_=sf_v[m]), sfsem[s3], 16)
                if m >= 1:
                    mm = m - 1
                    sc.wait_ge(s_v, b_v + 2 * mm + 1)
                    inc(sc.activation(tsc[1][:, 0:NT], tsc[0][:, 0:NT], AF.Identity,
                                      bias=bet[:, mm:mm + 1], scale=gam[:, mm:mm + 1]),
                        s_s, 1)

    root.close()
    return nc, tap_h


def _prep_inputs(L0_attn_inputs):
    x = np.asarray(L0_attn_inputs)
    shards = []
    for c in range(N_CORES):
        b, t0 = c // 4, (c % 4) * NT
        pad = np.zeros((NTH, D), dtype=np.float32)
        lo, hi = t0 - 1, t0 + NT + 1
        src_lo, src_hi = max(lo, 0), min(hi, 4096)
        pad[src_lo - lo: src_lo - lo + (src_hi - src_lo)] = x[b, src_lo:src_hi]
        shards.append(np.ascontiguousarray(pad.T))
    return shards


def kernel(**inputs):
    nc, _ = build_program()
    xs = _prep_inputs(inputs["L0_attn_inputs"])
    def pm(a, k):
        return np.ascontiguousarray(np.asarray(a, np.float32).reshape(k, 128).T)

    base = {
        "conv_w": np.ascontiguousarray(inputs["L0_attn_conv_w"], dtype=np.float32),
        "conv_b_pm": pm(inputs["L0_attn_conv_b"], 24),
        "wq": np.ascontiguousarray(inputs["L0_attn_wq"], dtype=np.float32),
        "bq_pm": pm(np.asarray(inputs["L0_attn_bq"]).reshape(-1), 8),
        "wk": np.ascontiguousarray(inputs["L0_attn_wk"], dtype=np.float32),
        "bk_pm": pm(np.asarray(inputs["L0_attn_bk"]).reshape(-1), 8),
        "wv": np.ascontiguousarray(inputs["L0_attn_wv"], dtype=np.float32),
        "bv": np.ascontiguousarray(inputs["L0_attn_bv"], dtype=np.float32),
        "wo": np.ascontiguousarray(inputs["L0_attn_wo"], dtype=np.float32),
        "bo_pm": pm(inputs["L0_attn_bo"], 8),
        "proj_w": np.ascontiguousarray(inputs["L0_attn_proj_w"], dtype=np.float32),
        "proj_b_pm": pm(inputs["L0_attn_proj_b"], 32),
        "gate_w_pm": np.ascontiguousarray(
            np.asarray(inputs["L0_ffn_gate_w"], np.float32).reshape(32, 128, 8)
            .transpose(1, 0, 2).reshape(128, 256)),
        "up_w": np.ascontiguousarray(inputs["L0_ffn_up_w"], dtype=np.float32),
        "down_w": np.ascontiguousarray(inputs["L0_ffn_down_w"], dtype=np.float32),
        "fc1_w": np.ascontiguousarray(inputs["L0_ffn_fc1_w"], dtype=np.float32),
        "fc2_w": np.ascontiguousarray(inputs["L0_ffn_fc2_w"], dtype=np.float32),
        "expert_w": np.ascontiguousarray(inputs["L0_ffn_expert_w"], dtype=np.float32),
        "ln_gamma_pm": pm(inputs["L0_ffn_ln_gamma"], 32),
        "ln_beta_pm": pm(inputs["L0_ffn_ln_beta"], 32),
    }
    in_maps = [dict(base, x_fm=xs[c]) for c in range(N_CORES)]
    res = run_bass_kernel_spmd(nc, in_maps, core_ids=list(range(N_CORES)))
    out = np.empty((2, 4096, 4096), dtype=np.float32)
    for c in range(N_CORES):
        b, t0 = c // 4, (c % 4) * NT
        out[b, t0:t0 + NT, :] = res.results[c]["out_fm"].T
    return out
